# version 21
# baseline (speedup 1.0000x reference)
from contextlib import ExitStack

import numpy as np

import concourse.bass as bass
import concourse.mybir as mybir
from concourse.bass_utils import run_bass_kernel_spmd

F32 = mybir.dt.float32
BF16 = mybir.dt.bfloat16
ALU = mybir.AluOpType
AX = mybir.AxisListType
ACT = mybir.ActivationFunctionType

T = 2048
D = 1024
PA = 4
TPA = 512
PF = 8
F = 4096
DKS = float(128 ** (-0.5))
EPS = 1e-5


def build_nc():
    nc = bass.Bass("TRN2", target_bir_lowering=False)

    x_hbm = nc.dram_tensor("x", [T, D], F32, kind="ExternalInput").ap()
    wq_h = nc.dram_tensor("Wq", [D, D], F32, kind="ExternalInput").ap()
    wk_h = nc.dram_tensor("Wk", [D, D], F32, kind="ExternalInput").ap()
    wv_h = nc.dram_tensor("Wv", [D, D], F32, kind="ExternalInput").ap()
    wo_h = nc.dram_tensor("Wo", [D, D], F32, kind="ExternalInput").ap()
    bq_h = nc.dram_tensor("bq", [D], F32, kind="ExternalInput").ap()
    bk_h = nc.dram_tensor("bk", [D], F32, kind="ExternalInput").ap()
    bv_h = nc.dram_tensor("bv", [D], F32, kind="ExternalInput").ap()
    bo_h = nc.dram_tensor("bo", [D], F32, kind="ExternalInput").ap()
    wfc_h = nc.dram_tensor("Wfc", [F, D], F32, kind="ExternalInput").ap()
    wpr_h = nc.dram_tensor("Wpr", [D, F], F32, kind="ExternalInput").ap()
    out_hbm = nc.dram_tensor("out", [T, D], F32, kind="ExternalOutput").ap()
    w2d_h = nc.dram_tensor("w2scratch", [32 * 128 * 1024], BF16,
                           kind="Internal").ap()
    w1d_h = nc.dram_tensor("w1scratch", [32 * 128 * 1024], BF16,
                           kind="Internal").ap()

    ctx = ExitStack()
    e = ctx.enter_context

    x_sb = e(nc.sbuf_tensor("x_sb", [128, T * D // 128], F32))
    regB = ExitStack()
    wq_sb = regB.enter_context(nc.sbuf_tensor("wq_sb", [128, 8192], BF16))
    wk_sb = regB.enter_context(nc.sbuf_tensor("wk_sb", [128, 8192], BF16))
    wv_sb = regB.enter_context(nc.sbuf_tensor("wv_sb", [128, 8192], BF16))
    wo_sb = regB.enter_context(nc.sbuf_tensor("wo_sb", [128, 8192], BF16))
    regC = ExitStack()
    xn1_bf = regC.enter_context(nc.sbuf_tensor("xn1_bf", [128, 4096], BF16))
    xn1t_sb = regC.enter_context(nc.sbuf_tensor("xn1t_sb", [128, 4096], BF16))
    q_bf = regC.enter_context(nc.sbuf_tensor("q_bf", [128, 4096], BF16))
    k_bf = regC.enter_context(nc.sbuf_tensor("k_bf", [128, 4096], BF16))
    v_bf = regC.enter_context(nc.sbuf_tensor("v_bf", [128, 4096], BF16))
    attn_sb = regC.enter_context(nc.sbuf_tensor("attn_sb", [128, 4096], BF16))
    s_f32 = [e(nc.sbuf_tensor(f"s_f32_{i}", [128, 256], F32, side="right"))
             for i in range(3)]
    p_bf = [e(nc.sbuf_tensor(f"p_bf_{i}", [128, 256], BF16, side="right"))
            for i in range(3)]
    pt_bf = [e(nc.sbuf_tensor(f"pt_bf_{i}", [128, 256], BF16, side="right"))
             for i in range(3)]
    ident = e(nc.sbuf_tensor("ident", [128, 128], BF16, side="right"))
    wtmp2 = [e(nc.sbuf_tensor(f"wtmp2_{i}", [128, 2048], BF16, side="right"))
             for i in range(2)]
    wtmp1 = [e(nc.sbuf_tensor(f"wtmp1_{i}", [128, 1024], BF16, side="right"))
             for i in range(4)]
    w2stage = [e(nc.sbuf_tensor(f"w2stage_{i}", [128, 1024], BF16,
                                side="right")) for i in range(2)]
    bo_bf = e(nc.sbuf_tensor("bo_bf", [1, 1024], BF16, side="right"))
    ones_bf = e(nc.sbuf_tensor("ones_bf", [1, 128], BF16, side="right"))
    bq_sb = e(nc.sbuf_tensor("bq_sb", [128, 8], F32, side="right"))
    bk_sb = e(nc.sbuf_tensor("bk_sb", [128, 8], F32, side="right"))
    bv_sb = e(nc.sbuf_tensor("bv_sb", [128, 8], F32, side="right"))
    bqs_sb = e(nc.sbuf_tensor("bqs_sb", [128, 8], F32, side="right"))
    sums = e(nc.sbuf_tensor("sums", [128, 4], F32, side="right"))
    sqsums = e(nc.sbuf_tensor("sqsums", [128, 4], F32, side="right"))
    mmean = e(nc.sbuf_tensor("mmean", [128, 4], F32, side="right"))
    nmean = e(nc.sbuf_tensor("nmean", [128, 4], F32, side="right"))
    vtmp = e(nc.sbuf_tensor("vtmp", [128, 4], F32, side="right"))
    veps = e(nc.sbuf_tensor("veps", [128, 4], F32, side="right"))
    rstd = e(nc.sbuf_tensor("rstd", [128, 4], F32, side="right"))
    negmax = e(nc.sbuf_tensor("negmax", [128, 4], F32, side="right"))
    ssum = e(nc.sbuf_tensor("ssum", [128, 4], F32, side="right"))
    rsum = e(nc.sbuf_tensor("rsum", [128, 4], F32, side="right"))

    psA = e(nc.psum_tensor("psA", [128, 1024], F32))
    psB = e(nc.psum_tensor("psB", [128, 1024], F32))
    psC = e(nc.psum_tensor("psC", [128, 1024], F32))
    psD = e(nc.psum_tensor("psD", [128, 2048], BF16))

    S = {nm: e(nc.semaphore(nm)) for nm in ("pe", "ve", "se", "gp")}
    NSY, NGP, NSE = 6, 10, 6
    for i in range(NSY):
        S[f"qs{i}"] = e(nc.semaphore(f"qs{i}"))
    for i in range(NGP):
        S[f"qg{i}"] = e(nc.semaphore(f"qg{i}"))
    for i in range(NSE):
        S[f"qe{i}"] = e(nc.semaphore(f"qe{i}"))

    plan = {k: [] for k in ("pe", "ve", "se", "gp", "sy")}
    n = dict(pe=0, ve=0, se=0, gp=0)
    for i in range(NSY):
        n[f"qs{i}"] = 0
    for i in range(NGP):
        n[f"qg{i}"] = 0
    for i in range(NSE):
        n[f"qe{i}"] = 0

    def emit(eng, fn):
        plan[eng].append(fn)

    def wait(eng, sem, val):
        if val <= 0:
            return
        assert val <= n[sem], (eng, sem, val, n[sem])
        emit(eng, lambda E, sem=sem, val=val: E.wait_ge(S[sem], val))

    def op(eng, build, inc=None):
        if inc is None:
            emit(eng, lambda E, b=build: b(E))
            return None
        n[inc] += 1
        emit(eng, lambda E, b=build, s=inc: b(E).then_inc(S[s], 1))
        return n[inc]

    qidx = {"sy": 0, "gp": 0, "se": 0}

    def dma(eng, build):
        npool = {"sy": NSY, "gp": NGP, "se": NSE}[eng]
        pfx = {"sy": "qs", "gp": "qg", "se": "qe"}[eng]
        k = qidx[eng] % npool
        qidx[eng] += 1
        nm = f"{pfx}{k}"
        if n[nm] > 0:
            wait(eng, nm, n[nm])
        n[nm] += 16
        emit(eng, lambda E, b=build, nm=nm: b(E).then_inc(S[nm], 16))
        return (nm, n[nm])

    op("gp", lambda E: E.iota(ident[:], [[1, 128]], base=0,
                              channel_multiplier=-1,
                              allow_small_or_imprecise_dtypes=True), inc="gp")
    wait("gp", "gp", n["gp"])
    op("gp", lambda E: E.tensor_single_scalar(ident[:], ident[:], 0.0,
                                              ALU.is_equal), inc="gp")
    E_ident = n["gp"]
    op("gp", lambda E: E.memset(ones_bf[:], 1.0), inc="gp")
    E_ones = n["gp"]

    E_xp = []
    E_xp.append(dma("sy", lambda E: E.dma_start(
        out=x_sb[:, 0:4096].rearrange("p (j d) -> p j d", d=D),
        in_=x_hbm[0:TPA, :].rearrange("(j p) d -> p j d", p=128))))
    E_bias = {}
    for nm2, bh, bsb in (("q", bq_h, bq_sb), ("k", bk_h, bk_sb),
                         ("v", bv_h, bv_sb)):
        E_bias[nm2] = dma("sy", lambda E, bh=bh, bsb=bsb: E.dma_start(
            out=bsb[:], in_=bass.AP(bh.tensor, 0, [[1, 128], [128, 8]]),
            allow_slow_non_contiguous=True))
    for p in range(1, PA):
        E_xp.append(dma("sy", lambda E, p=p: E.dma_start(
            out=x_sb[:, p * 4096:(p + 1) * 4096].rearrange(
                "p (j d) -> p j d", d=D),
            in_=x_hbm[p * TPA:(p + 1) * TPA, :].rearrange(
                "(j p) d -> p j d", p=128))))

    E_w = {}
    E_wq_half = dma("gp", lambda E: E.dma_start(
        out=bass.AP(wq_sb[:].tensor, 0, [[8192, 128], [1024, 8], [1, 512]]),
        in_=wq_h[:, 0:512].rearrange("(c p) d -> p c d", p=128)))
    E_w["q"] = dma("gp", lambda E: E.dma_start(
        out=bass.AP(wq_sb[:].tensor, 512, [[8192, 128], [1024, 8], [1, 512]]),
        in_=wq_h[:, 512:1024].rearrange("(c p) d -> p c d", p=128)))
    for nm, wh, wsb in (("k", wk_h, wk_sb), ("v", wv_h, wv_sb),
                        ("o", wo_h, wo_sb)):
        E_w[nm] = dma("gp", lambda E, wh=wh, wsb=wsb: E.dma_start(
            out=wsb[:].rearrange("p (c d) -> p c d", d=D),
            in_=wh.rearrange("(c p) d -> p c d", p=128)))
    E_bo = dma("gp", lambda E: E.dma_start(
        out=bo_bf[:], in_=bass.AP(bo_h.tensor, 0, [[0, 1], [1, 1024]])))

    chunks2 = [(m, fh) for m in range(8) for fh in range(2)]
    E_ld2 = {}
    E_tr2 = {}
    E_st2 = {}
    w2g = {"gidx": 0}

    def w2_dma(ci):
        m, fh = chunks2[ci]
        if ci >= 2:
            wait("gp", "pe", E_tr2[ci - 2])
        E_ld2[ci] = dma("gp", lambda E, m=m, fh=fh, ci=ci: E.dma_start(
            out=wtmp2[ci % 2][:],
            in_=wpr_h[m * 128:(m + 1) * 128, fh * 2048:(fh + 1) * 2048]))


    w2_dma(0)
    w2_dma(1)

    wait("ve", *E_bias["q"])
    wait("ve", *E_bias["k"])
    wait("ve", *E_bias["v"])
    op("ve", lambda E: E.tensor_single_scalar(bqs_sb[:], bq_sb[:], DKS,
                                              ALU.mult), inc="ve")
    E_bqs = n["ve"]

    E_w1ld = {}
    E_w1tr = {}

    def w1_dma(fc):
        if fc >= 4:
            wait("gp", "pe", E_w1tr[fc - 4])
        E_w1ld[fc] = dma("gp", lambda E, fc=fc: E.dma_start(
            out=wtmp1[fc % 4][:], in_=wfc_h[fc * 128:(fc + 1) * 128, :]))

    for fc in range(4):
        w1_dma(fc)

    relQ = [("ve", 0)] * 4
    relD = [("se", 0), ("se", 0)]
    relC2 = [("ve", 0), ("ve", 0)]

    qkv_quarters = [psA[:, 0:512], psA[:, 512:1024],
                    psB[:, 0:512], psB[:, 512:1024]]

    def pe_wait_rel(rel):
        wait("pe", rel[0], rel[1])

    E_x2 = {}
    war = {"tta": 0, "p": 0}

    E_xn1_d = {}
    E_xn1t_fill_prev = [0]

    def ln1_part1(p, jj):
        xtj = x_sb[:, (4 * p + jj) * 1024:(4 * p + jj + 1) * 1024]
        xoj = xn1_bf[:, jj * 1024:(jj + 1) * 1024]
        if jj == 0:
            wait("se", *E_xp[p])
            wait("se", "pe", E_xn1t_fill_prev[0])
        op("se", lambda E, xtj=xtj, xoj=xoj: E.activation(
            xoj, xtj, ACT.Identity, accum_out=sums[:, jj:jj + 1]),
            inc="se")
        wait("se", "se", n["se"])
        op("se", lambda E, xtj=xtj, xoj=xoj: E.activation(
            xoj, xtj, ACT.Square, accum_out=sqsums[:, jj:jj + 1]),
            inc="se")

    def ln1_part2(p):
        wait("ve", "se", n["se"])
        for b_ in [
            lambda E: E.tensor_single_scalar(
                nmean[:, :4], sums[:, :4], -1.0 / D, ALU.mult),
            lambda E: E.tensor_single_scalar(
                mmean[:, :4], sums[:, :4], 1.0 / D, ALU.mult),
            lambda E: E.tensor_mul(mmean[:, :4], mmean[:, :4], mmean[:, :4]),
            lambda E: E.scalar_tensor_tensor(
                veps[:, :4], sqsums[:, :4], 1.0 / D, mmean[:, :4],
                ALU.mult, ALU.subtract),
            lambda E: E.tensor_single_scalar(
                veps[:, :4], veps[:, :4], EPS, ALU.add),
        ]:
            op("ve", b_, inc="ve")
            wait("ve", "ve", n["ve"])
        wait("se", "ve", n["ve"])
        op("se", lambda E: E.activation(vtmp[:, :4], veps[:, :4], ACT.Ln),
           inc="se")
        wait("se", "se", n["se"])
        op("se", lambda E: E.activation(rstd[:, :4], vtmp[:, :4], ACT.Exp,
                                        scale=-0.5), inc="se")

    def ln1_part3(p):
        wait("ve", "se", n["se"])
        wait("ve", "pe", E_xn1t_fill_prev[0])
        for jj in range(4):
            xtj = x_sb[:, (4 * p + jj) * 1024:(4 * p + jj + 1) * 1024]
            xoj = xn1_bf[:, jj * 1024:(jj + 1) * 1024]
            op("ve", lambda E, jj=jj, xtj=xtj, xoj=xoj: E.tensor_scalar(
                xoj, xtj, nmean[:, jj:jj + 1], rstd[:, jj:jj + 1],
                ALU.add, ALU.mult), inc="ve" if jj == 3 else None)
        E_xn1_d[p] = n["ve"]

    for jj in range(4):
        ln1_part1(0, jj)
    ln1_part2(0)
    ln1_part3(0)

    E_xn1t_d = {}

    def xn1t_group(p, jj):
        if jj == 0:
            wait("pe", "ve", E_xn1_d[p])
            if p == 0:
                wait("pe", "gp", E_ident)
        pe_wait_rel(relD[jj % 2])
        for c in range(8):
            op("pe", lambda E, jj=jj, c=c: E.transpose(
                psD[:, (jj % 2) * 1024 + c * 128:
                    (jj % 2) * 1024 + (c + 1) * 128],
                xn1_bf[:, jj * 1024 + c * 128:jj * 1024 + (c + 1) * 128],
                ident[:]), inc="pe" if c == 7 else None)
        fill = n["pe"]
        wait("se", "pe", fill)
        dst = bass.AP(xn1t_sb[:].tensor, jj * 128,
                      [[4096, 128], [512, 8], [1, 128]])
        src = psD[:, (jj % 2) * 1024:(jj % 2) * 1024 + 1024].rearrange(
            "p (c t) -> p c t", c=8)
        op("se", lambda E, dst=dst, src=src: E.copy(dst, src),
           inc="se")
        relD[jj % 2] = ("se", n["se"])
        E_xn1t_d[(p, jj)] = n["se"]
        if jj == 3:
            E_xn1t_fill_prev[0] = fill

    for p in range(PA):
        if p == 0:
            for jj in range(4):
                xn1t_group(0, jj)

        E_qe, E_ke = [0] * 8, [0] * 8
        wait("pe", "se", E_xn1t_d[(p, 3)])
        if p == 0:
            wait("pe", *E_wq_half)
        g = 0
        for oc in range(8):
            if p == 0 and oc == 4:
                wait("pe", *E_w["q"])
            qb = qkv_quarters[g % 4]
            pe_wait_rel(relQ[g % 4])
            for c in range(8):
                op("pe", lambda E, oc=oc, c=c, qb=qb: E.matmul(
                    qb, wq_sb[:, c * 1024 + oc * 128:c * 1024 + (oc + 1) * 128],
                    xn1t_sb[:, c * 512:(c + 1) * 512],
                    start=(c == 0), stop=(c == 7)),
                    inc="pe" if c == 7 else None)
            wait("se", "pe", n["pe"])
            if p == 0 and oc == 0:
                wait("se", "ve", E_bqs)
            op("se", lambda E, oc=oc, qb=qb: E.activation(
                q_bf[:, oc * 512:(oc + 1) * 512], qb, ACT.Identity,
                bias=bqs_sb[:, oc:oc + 1], scale=DKS), inc="se")
            relQ[g % 4] = ("se", n["se"])
            E_qe[oc] = n["se"]
            g += 1
        if p == 0:
            wait("pe", *E_w["k"])
        for oc in range(8):
            qb = qkv_quarters[g % 4]
            pe_wait_rel(relQ[g % 4])
            for c in range(8):
                op("pe", lambda E, oc=oc, c=c, qb=qb: E.matmul(
                    qb, wk_sb[:, c * 1024 + oc * 128:c * 1024 + (oc + 1) * 128],
                    xn1t_sb[:, c * 512:(c + 1) * 512],
                    start=(c == 0), stop=(c == 7)),
                    inc="pe" if c == 7 else None)
            wait("se", "pe", n["pe"])
            op("se", lambda E, oc=oc, qb=qb: E.activation(
                k_bf[:, oc * 512:(oc + 1) * 512], qb, ACT.Identity,
                bias=bk_sb[:, oc:oc + 1]), inc="se")
            relQ[g % 4] = ("se", n["se"])
            E_ke[oc] = n["se"]
            g += 1
        if p == 0:
            wait("pe", *E_w["v"])
        for jj in range(4):
            for half in range(2):
                qb = qkv_quarters[g % 4]
                pe_wait_rel(relQ[g % 4])
                for c in range(8):
                    op("pe", lambda E, jj=jj, half=half, c=c, qb=qb: E.matmul(
                        qb,
                        xn1t_sb[:, c * 512 + jj * 128:c * 512 + (jj + 1) * 128],
                        wv_sb[:, c * 1024 + half * 512:
                              c * 1024 + (half + 1) * 512],
                        start=(c == 0), stop=(c == 7)),
                        inc="pe" if c == 7 else None)
                wait("ve", "pe", n["pe"])
                op("ve", lambda E, jj=jj, half=half, qb=qb: E.tensor_copy(
                    v_bf[:, jj * 1024 + half * 512:
                         jj * 1024 + (half + 1) * 512], qb), inc="ve")
                relQ[g % 4] = ("ve", n["ve"])
                g += 1
        E_v_all = n["ve"]

        E_tta = [0] * 8
        E_p = [0] * 8
        E_exph = [0] * 8
        E_ptev = [0] * 8
        E_pt = [0] * 8
        E_pv = [0] * 8
        E_attnev = [0] * 8

        def scores(h):
            wait("pe", "se", max(E_qe[h], E_ke[h]))
            if h >= 2:
                wait("pe", "ve", E_tta[h - 2])
            else:
                pe_wait_rel(relC2[h % 2])
            for bl in range(8):
                op("pe", lambda E, h=h, bl=bl: E.matmul(
                    psC[(bl // 4) * 64:(bl // 4) * 64 + 64,
                        (h % 2) * 512 + (bl % 4) * 64:
                        (h % 2) * 512 + (bl % 4) * 64 + 64],
                    q_bf[:, h * 512 + bl * 64:h * 512 + (bl + 1) * 64],
                    k_bf[:, h * 512 + bl * 64:h * 512 + (bl + 1) * 64],
                    start=True, stop=True), inc="pe" if bl == 7 else None)
            return n["pe"]

        def softmax_pre(h, E_s):
            sv = psC[:, (h % 2) * 512:(h % 2) * 512 + 256]
            g3 = sv.rearrange("p (g k) -> p g k", g=4)
            wait("ve", "pe", E_s)
            wait("ve", "ve", war["tta"])
            op("ve", lambda E, g3=g3: E.tensor_reduce(
                negmax[:, :4], g3, AX.X, ALU.max, negate=True), inc="ve")
            wait("ve", "ve", n["ve"])
            op("ve", lambda E, h=h, g3=g3: E.tensor_tensor(
                s_f32[h % 3][:].rearrange("p (g k) -> p g k", g=4), g3,
                negmax[:, :4].unsqueeze(2).broadcast_to([128, 4, 64]),
                ALU.add), inc="ve")
            E_tta[h] = n["ve"]
            war["tta"] = n["ve"]

        def exp_h(h):
            wait("se", "ve", E_tta[h])
            op("se", lambda E, h=h: E.activation(
                s_f32[h % 3][:], s_f32[h % 3][:], ACT.Exp), inc="se")
            E_exph[h] = n["se"]

        def softmax_post(h):
            wait("ve", "se", E_exph[h])
            wait("ve", "ve", war["p"])
            if h >= 3:
                wait("ve", "pe", E_pt[h - 3])
            g3 = s_f32[h % 3][:].rearrange("p (g k) -> p g k", g=4)
            op("ve", lambda E, g3=g3: E.tensor_reduce(
                ssum[:, :4], g3, AX.X, ALU.add), inc="ve")
            wait("ve", "ve", n["ve"])
            op("ve", lambda E: E.reciprocal(rsum[:, :4], ssum[:, :4]),
               inc="ve")
            wait("ve", "ve", n["ve"])
            op("ve", lambda E, h=h, g3=g3: E.tensor_tensor(
                p_bf[h % 3][:].rearrange("p (g k) -> p g k", g=4), g3,
                rsum[:, :4].unsqueeze(2).broadcast_to([128, 4, 64]),
                ALU.mult), inc="ve")
            E_p[h] = n["ve"]
            war["p"] = n["ve"]

        def pt_h(h):
            wait("pe", "ve", E_p[h])
            pe_wait_rel(relD[h % 2])
            base = (h % 2) * 1024
            for t_ in range(2):
                op("pe", lambda E, h=h, t_=t_, base=base: E.transpose(
                    psD[:, base + t_ * 128:base + (t_ + 1) * 128],
                    p_bf[h % 3][:, t_ * 128:(t_ + 1) * 128], ident[:]),
                    inc="pe" if t_ == 1 else None)
            E_pt[h] = n["pe"]

        def ptev_h(h):
            wait("se", "pe", E_pt[h])
            base = (h % 2) * 1024
            op("se", lambda E, h=h, base=base: E.copy(
                pt_bf[h % 3][:], psD[:, base:base + 256]), inc="se")
            E_ptev[h] = n["se"]
            relD[h % 2] = ("se", n["se"])

        def pv_h(h):
            wait("pe", "se", E_ptev[h])
            if h < 2:
                wait("pe", "ve", E_v_all)
                wait("pe", "se", max(E_qe[7], E_ke[7]))
            else:
                wait("pe", "ve", E_attnev[h - 2])
            pair = psA if h % 2 == 0 else psB
            for bl in range(8):
                panel = bl % 4
                tt = panel // 2
                kb = (panel % 2) * 64
                qc = (bl // 4) * 64
                col = (bl % 2) * 512 + (bl // 2) * 64
                op("pe", lambda E, h=h, bl=bl, tt=tt, kb=kb, qc=qc, col=col,
                   pair=pair: E.matmul(
                       pair[:, col:col + 64],
                       v_bf[(bl % 2) * 64:(bl % 2) * 64 + 64,
                            (bl // 2) * 1024 + h * 128:
                            (bl // 2) * 1024 + (h + 1) * 128],
                       pt_bf[h % 3][kb:kb + 64,
                                    tt * 128 + qc:tt * 128 + qc + 64],
                       start=True, stop=True),
                    inc="pe" if bl == 7 else None)
            E_pv[h] = n["pe"]

        def attnev_h(h):
            wait("ve", "pe", E_pv[h])
            pair = psA if h % 2 == 0 else psB
            src = bass.AP(pair[:].tensor, 0,
                          [[1024, 128], [512, 2], [64, 4], [1, 64]])
            dst = bass.AP(attn_sb[:].tensor, h * 512,
                          [[4096, 128], [64, 2], [128, 4], [1, 64]])
            op("ve", lambda E, h=h, src=src, dst=dst: E.tensor_scalar(
                dst, src, bv_sb[:, h:h + 1], None, ALU.add), inc="ve")
            E_attnev[h] = n["ve"]

        def w1_group(fc):
            half = w2g["gidx"] % 2
            pe_wait_rel(relD[half])
            for c in range(8):
                op("pe", lambda E, fc=fc, c=c, half=half: E.transpose(
                    psD[:, half * 1024 + c * 128:half * 1024 + (c + 1) * 128],
                    wtmp1[fc % 4][:, c * 128:(c + 1) * 128], ident[:]),
                    inc="pe" if c == 7 else None)
            fill = n["pe"]
            E_w1tr[fc] = n["pe"]
            if fc + 4 < 32:
                w1_dma(fc + 4)
            wait("se", "pe", fill)
            if w2g["gidx"] >= 2:
                wait("se", *E_st2[w2g["gidx"] - 2])
            op("se", lambda E, half=half, s=w2g["gidx"] % 2: E.copy(
                w2stage[s][:], psD[:, half * 1024:half * 1024 + 1024]),
                inc="se")
            relD[half] = ("se", n["se"])
            wait("se", "se", n["se"])
            dst = bass.AP(w1d_h.tensor, fc * 131072, [[1024, 128], [1, 1024]])
            E_st2[w2g["gidx"]] = dma("se", lambda E, dst=dst,
                                     s=w2g["gidx"] % 2:
                                     E.dma_start(out=dst, in_=w2stage[s][:]))
            w2g["gidx"] += 1

        def w2_group(ci, grp):
            half = w2g["gidx"] % 2
            pe_wait_rel(relD[half])
            for t_ in range(8):
                tl = grp * 8 + t_
                op("pe", lambda E, ci=ci, tl=tl, half=half: E.transpose(
                    psD[:, half * 1024 + (tl % 8) * 128:
                        half * 1024 + (tl % 8 + 1) * 128],
                    wtmp2[ci % 2][:, tl * 128:(tl + 1) * 128], ident[:]),
                    inc="pe" if t_ == 7 else None)
            fill = n["pe"]
            m, fh = chunks2[ci]
            wait("ve", "pe", fill)
            if w2g["gidx"] >= 2:
                wait("ve", *E_st2[w2g["gidx"] - 2])
            op("ve", lambda E, half=half, s=w2g["gidx"] % 2: E.tensor_copy(
                w2stage[s][:], psD[:, half * 1024:half * 1024 + 1024]),
                inc="ve")
            relD[half] = ("ve", n["ve"])
            base = (fh * 16 + grp * 8) * 131072 + m * 128
            dst = bass.AP(w2d_h.tensor, base,
                          [[1024, 128], [131072, 8], [1, 128]])
            wait("sy", "ve", n["ve"])
            E_st2[w2g["gidx"]] = dma("sy", lambda E, dst=dst,
                                     s=w2g["gidx"] % 2:
                                     E.dma_start(out=dst, in_=w2stage[s][:]))
            w2g["gidx"] += 1
            if grp == 1:
                E_tr2[ci] = n["pe"]
                if ci + 2 < len(chunks2):
                    w2_dma(ci + 2)

        E_s0 = scores(0)
        softmax_pre(0, E_s0)
        exp_h(0)
        E_s1 = scores(1)
        softmax_pre(1, E_s1)
        exp_h(1)
        softmax_post(0)
        pt_h(0)
        ptev_h(0)
        E_s2 = scores(2)
        softmax_pre(2, E_s2)
        exp_h(2)
        softmax_post(1)
        pt_h(1)
        ptev_h(1)
        for h in range(8):
            if h + 3 < 8:
                E_s = scores(h + 3)
                softmax_pre(h + 3, E_s)
                exp_h(h + 3)
            if h + 2 < 8:
                softmax_post(h + 2)
                pt_h(h + 2)
                ptev_h(h + 2)
            pv_h(h)
            attnev_h(h)
            ci = 4 * p + h // 2
            if h % 2 == 0:
                wait("pe", *E_ld2[ci])
            w2_group(ci, h % 2)
            fcw = 8 * p + h
            wait("pe", *E_w1ld[fcw])
            w1_group(fcw)
            if p + 1 < PA:
                if 1 <= h <= 4:
                    ln1_part1(p + 1, h - 1)
                elif h == 5:
                    ln1_part2(p + 1)
                elif h == 6:
                    ln1_part3(p + 1)
        relC2[0] = ("ve", E_tta[6])
        relC2[1] = ("ve", E_tta[7])

        if p == 0:
            wait("pe", *E_bo)
            wait("pe", *E_w["o"])
            wait("pe", "gp", E_ones)
        wait("pe", "ve", E_attnev[7])
        for jj in range(4):
            pair = psA if jj % 2 == 0 else psB
            if jj >= 2:
                wait("pe", "ve", E_x2[4 * p + jj - 2])
            for half in range(2):
                for c in range(8):
                    op("pe", lambda E, jj=jj, half=half, c=c, pair=pair:
                       E.matmul(
                           pair[:, half * 512:(half + 1) * 512],
                           attn_sb[:, c * 512 + jj * 128:
                                   c * 512 + (jj + 1) * 128],
                           wo_sb[:, c * 1024 + half * 512:
                                 c * 1024 + (half + 1) * 512],
                           start=(c == 0), stop=False))
                op("pe", lambda E, half=half, pair=pair: E.matmul(
                    pair[:, half * 512:(half + 1) * 512], ones_bf[:],
                    bo_bf[0:1, half * 512:(half + 1) * 512],
                    start=False, stop=True), inc="pe" if half == 1 else None)
            E_opj = n["pe"]
            if p + 1 < PA:
                xn1t_group(p + 1, jj)
            wait("ve", "pe", E_opj)
            jg = 4 * p + jj
            for half in range(2):
                xs = x_sb[:, jg * 1024 + half * 512:jg * 1024 + (half + 1) * 512]
                op("ve", lambda E, xs=xs, pair=pair, half=half:
                   E.scalar_tensor_tensor(
                       xs, pair[:, half * 512:(half + 1) * 512], 1.0, xs,
                       ALU.mult, ALU.add), inc="ve" if half == 1 else None)
            E_x2[jg] = n["ve"]
        relQ[0] = relQ[1] = ("ve", E_x2[4 * p + 2])
        relQ[2] = relQ[3] = ("ve", E_x2[4 * p + 3])

    E_x2_all = n["ve"]
    E_attn_end_pe = n["pe"]

    regC.close()
    regB.close()
    E_w2_all = [(f"qe{i}", n[f"qe{i}"]) for i in range(NSE)]

    xn2_bf = [e(nc.sbuf_tensor(f"xn2_bf_{i}", [128, 1024], BF16))
              for i in range(2)]
    xn2t_sb = e(nc.sbuf_tensor("xn2t_sb", [128, 2048], BF16))
    h_bf = [e(nc.sbuf_tensor(f"h_bf_{i}", [128, 256], BF16)) for i in range(2)]
    w2slab = [e(nc.sbuf_tensor(f"w2slab_{i}", [128, 2048], BF16))
              for i in range(4)]
    w1slab = [e(nc.sbuf_tensor(f"w1slab_{i}", [128, 2048], BF16))
              for i in range(4)]
    y_sb = e(nc.sbuf_tensor("y_sb", [128, 2048], F32))
    mem_q = e(nc.sbuf_tensor("mem_q", [128, 2048], F32))
    spk_q = e(nc.sbuf_tensor("spk_q", [128, 2048], BF16))
    acc_q = e(nc.sbuf_tensor("acc_q", [128, 2048], BF16))

    relH = relC2
    relD6 = [relD[0]]
    relD7 = [relD[1]]
    E_slab1 = {}

    def slab1_load(p, pr, prev_fc):
        if p == 0 and pr == 0:
            for sm, vv in E_w2_all:
                wait("sy", sm, vv)
        if prev_fc:
            wait("sy", "pe", prev_fc)
        ev = dma("sy", lambda E, pr=pr: E.dma_start(
            out=w1slab[pr % 4][:].rearrange("p (f c) -> p f c", f=2),
            in_=bass.AP(w1d_h.tensor, pr * 262144,
                        [[1024, 128], [131072, 2], [1, 1024]])))
        E_slab1[(p, 2 * pr)] = ev
        E_slab1[(p, 2 * pr + 1)] = ev

    E_yev = {}
    E_spk_yfree = {}
    E_spk_vdone = {}
    E_spk_gdone = {}
    E_final = {}
    E_accT = {}
    E_w1ld = {}
    E_w1ev = {}
    E_xn2t = {}
    E_norm2_d = {}
    E_proj_pe = {}
    E_slab = {}
    E_xn2t_fill = [0, 0]

    def ln2_tile(p, jj):
        jg = 2 * p + jj
        xs = x_sb[:, jg * 1024:(jg + 1) * 1024]
        if p == 0 and jj == 0:
            wait("ve", "ve", E_x2_all)
            wait("se", "ve", E_x2_all)
        op("ve", lambda E, xs=xs: E.reduce_sum(sums[:, 0:1], xs, AX.X),
           inc="ve")
        E_red = n["ve"]
        wait("se", "pe", E_xn2t_fill[jj])
        op("se", lambda E, xs=xs, jj=jj: E.activation(
            xn2_bf[jj][:], xs, ACT.Square, accum_out=sqsums[:, 0:1]),
            inc="se")
        E_sq2 = n["se"]
        wait("ve", "ve", E_red)
        wait("ve", "se", E_sq2)
        for b_ in [
            lambda E: E.tensor_single_scalar(
                nmean[:, 0:1], sums[:, 0:1], -1.0 / D, ALU.mult),
            lambda E: E.tensor_single_scalar(
                mmean[:, 0:1], sums[:, 0:1], 1.0 / D, ALU.mult),
            lambda E: E.tensor_mul(mmean[:, 0:1], mmean[:, 0:1],
                                   mmean[:, 0:1]),
            lambda E: E.scalar_tensor_tensor(
                veps[:, 0:1], sqsums[:, 0:1], 1.0 / D, mmean[:, 0:1],
                ALU.mult, ALU.subtract),
            lambda E: E.tensor_single_scalar(
                veps[:, 0:1], veps[:, 0:1], EPS, ALU.add),
        ]:
            op("ve", b_, inc="ve")
            wait("ve", "ve", n["ve"])
        E_veps2 = n["ve"]
        wait("se", "ve", E_veps2)
        op("se", lambda E: E.activation(vtmp[:, 0:1], veps[:, 0:1],
                                        ACT.Ln), inc="se")
        wait("se", "se", n["se"])
        op("se", lambda E: E.activation(rstd[:, 0:1], vtmp[:, 0:1],
                                        ACT.Exp, scale=-0.5), inc="se")
        E_rstd2 = n["se"]
        wait("ve", "se", E_rstd2)
        op("ve", lambda E, xs=xs, jj=jj: E.tensor_scalar(
            xn2_bf[jj][:], xs, nmean[:, 0:1], rstd[:, 0:1],
            ALU.add, ALU.mult), inc="ve")
        E_norm2_d[(p, jj)] = n["ve"]

    def ln2_transposes(p, jj):
        wait("pe", "ve", E_norm2_d[(p, jj)])
        pe_wait_rel(relD6[0])
        for c in range(8):
            op("pe", lambda E, jj=jj, c=c: E.transpose(
                psD[:, c * 128:(c + 1) * 128],
                xn2_bf[jj][:, c * 128:(c + 1) * 128], ident[:]),
                inc="pe" if c == 7 else None)
        E_xn2t_fill[jj] = n["pe"]
        wait("se", "pe", E_xn2t_fill[jj])
        dst = bass.AP(xn2t_sb[:].tensor, jj * 128,
                      [[2048, 128], [256, 8], [1, 128]])
        src2 = psD[:, 0:1024].rearrange("p (c t) -> p c t", c=8)
        op("se", lambda E, dst=dst, src2=src2: E.copy(dst, src2),
           inc="se")
        relD6[0] = ("se", n["se"])
        E_xn2t[p] = n["se"]

    def slab_load(p, pr, prev_proj):
        if p == 0 and pr == 0:
            for sm, vv in E_w2_all:
                wait("sy", sm, vv)
        if prev_proj:
            wait("sy", "pe", prev_proj)
        ev = dma("sy", lambda E, pr=pr: E.dma_start(
            out=w2slab[pr % 4][:].rearrange("p (f c) -> p f c", f=2),
            in_=bass.AP(w2d_h.tensor, pr * 262144,
                        [[1024, 128], [131072, 2], [1, 1024]])))
        E_slab[(p, 2 * pr)] = ev
        E_slab[(p, 2 * pr + 1)] = ev

    ln2_tile(0, 0)
    ln2_tile(0, 1)
    for pr in range(4):
        slab_load(0, pr, 0)
        slab1_load(0, pr, 0)

    def emit_accT(pp, gd=None):
        for jj in range(2):
            wait("pe", "gp", E_spk_gdone[pp] if gd is None else gd[jj])
            prev = E_final.get((pp, jj - 1), E_final.get((pp - 1, 1), 0))
            wait("pe", "ve", prev)
            for c in range(8):
                op("pe", lambda E, jj=jj, c=c: E.transpose(
                    psD[:, 1024 + c * 128:1024 + (c + 1) * 128],
                    acc_q[:, c * 256 + jj * 128:c * 256 + (jj + 1) * 128],
                    ident[:]), inc="pe" if c == 7 else None)
            fill = n["pe"]
            jg = 2 * pp + jj
            xs = x_sb[:, jg * 1024:(jg + 1) * 1024]
            wait("ve", "pe", fill)
            op("ve", lambda E, xs=xs: E.scalar_tensor_tensor(
                xs, psD[:, 1024:2048], 0.2, xs, ALU.mult, ALU.add), inc="ve")
            E_final[(pp, jj)] = n["ve"]
            wait("gp", "ve", n["ve"])
            dma("gp", lambda E, xs=xs, jg=jg: E.dma_start(
                out=out_hbm[jg * 128:(jg + 1) * 128, :], in_=xs))
        E_accT[pp] = n["pe"]

    for p in range(PF):
        E_sqf = {}
        ln2_transposes(p, 0)
        ln2_transposes(p, 1)
        wait("pe", "se", E_xn2t[p])
        for step in range(33):
            fc = step
            if fc < 32:
                eng, cnt = relH[fc % 2]
                if cnt > 0:
                    wait("pe", eng, cnt)
                wait("pe", *E_slab1[(p, fc)])
                hp = psC[:, (fc % 2) * 512:(fc % 2) * 512 + 256]
                s1b = w1slab[(fc // 2) % 4][:, (fc % 2) * 1024:
                                            (fc % 2) * 1024 + 1024]
                for c in range(8):
                    op("pe", lambda E, fc=fc, c=c, hp=hp, s1b=s1b: E.matmul(
                        hp, s1b[:, c * 128:(c + 1) * 128],
                        xn2t_sb[:, c * 256:(c + 1) * 256],
                        start=(c == 0), stop=(c == 7)),
                        inc="pe" if c == 7 else None)
                E_fcf = n["pe"]
                if fc % 2 == 1 and fc // 2 + 4 < 16:
                    slab1_load(p, fc // 2 + 4, E_fcf)
                if p + 1 < PF and fc in (24, 28):
                    ln2_tile(p + 1, (fc - 24) // 4)
                wait("se", "pe", E_fcf)
                op("se", lambda E, fc=fc, hp=hp: E.activation(
                    h_bf[fc % 2][:], hp, ACT.Relu), inc="se")
                wait("se", "se", n["se"])
                op("se", lambda E, fc=fc: E.activation(
                    h_bf[fc % 2][:], h_bf[fc % 2][:], ACT.Square), inc="se")
                E_sqf[fc] = n["se"]
                relH[fc % 2] = ("se", E_sqf[fc])
            if step >= 1:
                fp = step - 1
                wait("pe", "se", E_sqf[fp])
                wait("pe", *E_slab[(p, fp)])
                if fp == 0:
                    if p > 0:
                        for q in range(4):
                            wait("pe", "se", E_yev[(p - 1, q)])
                    else:
                        wait("pe", "ve", E_x2_all)
                slb = w2slab[(fp // 2) % 4][:, (fp % 2) * 1024:
                                            (fp % 2) * 1024 + 1024]
                for m in range(8):
                    tgt = psA if m < 4 else psB
                    mc = m % 4
                    op("pe", lambda E, fp=fp, m=m, tgt=tgt, mc=mc, slb=slb:
                       E.matmul(
                           tgt[:, mc * 256:(mc + 1) * 256],
                           slb[:, m * 128:(m + 1) * 128],
                           h_bf[fp % 2][:],
                           start=(fp == 0 and m % 2 == 0),
                           stop=(fp == 31 and m % 2 == 1)),
                       inc="pe" if m == 7 else None)
                E_proj_pe[(p, fp)] = n["pe"]
                if fp % 2 == 1 and fp // 2 + 4 < 16:
                    slab_load(p, fp // 2 + 4, E_proj_pe[(p, fp)])
        E_y_fill = n["pe"]
        if p + 1 < PF:
            for j in range(4):
                slab_load(p + 1, j, E_proj_pe[(p, min(25 + 2 * j, 31))])
                slab1_load(p + 1, j, E_y_fill)
        if p > 0:
            emit_accT(p - 1)

        for q in range(4):
            ps = psA if q < 2 else psB
            half = q % 2
            wait("se", "pe", E_y_fill)
            if (p - 1) in E_spk_yfree:
                wait("se", "ve", E_spk_yfree[p - 1])
            op("se", lambda E, ps=ps, half=half, q=q: E.copy(
                y_sb[:, q * 512:(q + 1) * 512],
                ps[:, half * 512:(half + 1) * 512]), inc="se")
            E_yev[(p, q)] = n["se"]

        wait("ve", "se", E_yev[(p, 3)])
        if (p - 1) in E_spk_vdone:
            wait("ve", "ve", E_spk_vdone[p - 1])
        if (p - 1) in E_spk_gdone:
            wait("ve", "gp", E_spk_gdone[p - 1])

        def vop(build):
            op("ve", build, inc="ve")
            wait("ve", "ve", n["ve"])

        nhalf = 2 if p == PF - 1 else 1
        E_spk_gd_half = []

        def ap3(t, hf):
            return bass.AP(t[:].tensor, hf * 128,
                           [[2048, 128], [256, 8], [1, 128]])

        for hf in range(nhalf):
            if nhalf == 1:
                yv, sv, mv, av = y_sb[:], spk_q[:], mem_q[:], acc_q[:]
            else:
                yv, sv, mv, av = (ap3(y_sb, hf), ap3(spk_q, hf),
                                  ap3(mem_q, hf), ap3(acc_q, hf))
            vop(lambda E, yv=yv, sv=sv: E.tensor_single_scalar(
                sv, yv, 1.0, ALU.is_gt))
            if hf == 0 and (p - 1) in E_accT:
                wait("gp", "pe", E_accT[p - 1])
            wait("gp", "gp", n["gp"])
            wait("gp", "ve", n["ve"])
            op("gp", lambda E, sv=sv, av=av: E.tensor_copy(av, sv), inc="gp")
            vop(lambda E, yv=yv, sv=sv, mv=mv: E.scalar_tensor_tensor(
                mv, yv, 1.9, sv, ALU.mult, ALU.subtract))
            for st3 in range(3):
                wait("ve", "gp", n["gp"])
                vop(lambda E, sv=sv, mv=mv: E.tensor_single_scalar(
                    sv, mv, 1.0, ALU.is_gt))
                wait("gp", "gp", n["gp"])
                wait("gp", "ve", n["ve"])
                op("gp", lambda E, sv=sv, av=av: E.tensor_add(av, av, sv),
                   inc="gp")
                vop(lambda E, sv=sv, mv=mv: E.scalar_tensor_tensor(
                    mv, mv, 0.9, sv, ALU.mult, ALU.subtract))
                vop(lambda E, yv=yv, mv=mv: E.scalar_tensor_tensor(
                    mv, mv, 1.0, yv, ALU.mult, ALU.add))
                if st3 == 2 and hf == nhalf - 1:
                    E_spk_yfree[p] = n["ve"]
            wait("ve", "gp", n["gp"])
            vop(lambda E, sv=sv, mv=mv: E.tensor_single_scalar(
                sv, mv, 1.0, ALU.is_gt))
            E_spk_vdone[p] = n["ve"]
            wait("gp", "gp", n["gp"])
            wait("gp", "ve", n["ve"])
            op("gp", lambda E, sv=sv, av=av: E.tensor_add(av, av, sv),
               inc="gp")
            E_spk_gd_half.append(n["gp"])
        E_spk_gdone[p] = n["gp"]

        if p == PF - 1:
            emit_accT(p, E_spk_gd_half)

    for i in range(NSY):
        wait("sy", f"qs{i}", n[f"qs{i}"])
    for i in range(NGP):
        wait("sy", f"qg{i}", n[f"qg{i}"])
    for i in range(NSE):
        wait("sy", f"qe{i}", n[f"qe{i}"])

    with nc.Block() as block:
        @block.tensor
        def _(E):
            for fn in plan["pe"]:
                fn(E)

        @block.vector
        def _(E):
            for fn in plan["ve"]:
                fn(E)

        @block.scalar
        def _(E):
            for fn in plan["se"]:
                fn(E)

        @block.gpsimd
        def _(E):
            for fn in plan["gp"]:
                fn(E)

        @block.sync
        def _(E):
            for fn in plan["sy"]:
                fn(E)

    ctx.close()
    return nc


_NC_CACHE = None


def get_nc():
    global _NC_CACHE
    if _NC_CACHE is None:
        _NC_CACHE = build_nc()
    return _NC_CACHE


def kernel(L0_attn_src, L0_attn_Wq, L0_attn_bq, L0_attn_Wk, L0_attn_bk,
           L0_attn_Wv, L0_attn_bv, L0_attn_Wo, L0_attn_bo,
           L0_ffn_W_fc, L0_ffn_W_proj):
    nc = get_nc()
    x = np.ascontiguousarray(np.asarray(L0_attn_src, dtype=np.float32))
    common = {
        "Wq": np.ascontiguousarray(np.asarray(L0_attn_Wq, np.float32)),
        "Wk": np.ascontiguousarray(np.asarray(L0_attn_Wk, np.float32)),
        "Wv": np.ascontiguousarray(np.asarray(L0_attn_Wv, np.float32)),
        "Wo": np.ascontiguousarray(np.asarray(L0_attn_Wo, np.float32)),
        "bq": np.ascontiguousarray(np.asarray(L0_attn_bq, np.float32)),
        "bk": np.ascontiguousarray(np.asarray(L0_attn_bk, np.float32)),
        "bv": np.ascontiguousarray(np.asarray(L0_attn_bv, np.float32)),
        "bo": np.ascontiguousarray(np.asarray(L0_attn_bo, np.float32)),
        "Wfc": np.ascontiguousarray(np.asarray(L0_ffn_W_fc, np.float32)),
        "Wpr": np.ascontiguousarray(np.asarray(L0_ffn_W_proj, np.float32)),
    }
    in_maps = [dict(common, x=np.ascontiguousarray(x[c])) for c in range(8)]
    res = run_bass_kernel_spmd(nc, in_maps, core_ids=list(range(8)))
    return np.stack([res.results[c]["out"] for c in range(8)], axis=0)


# version 22
# speedup vs baseline: 1.0198x; 1.0198x over previous
from contextlib import ExitStack

import numpy as np

import concourse.bass as bass
import concourse.mybir as mybir
from concourse.bass_utils import run_bass_kernel_spmd

F32 = mybir.dt.float32
BF16 = mybir.dt.bfloat16
ALU = mybir.AluOpType
AX = mybir.AxisListType
ACT = mybir.ActivationFunctionType

T = 2048
D = 1024
PA = 4
TPA = 512
PF = 8
F = 4096
DKS = float(128 ** (-0.5))
EPS = 1e-5


def build_nc():
    nc = bass.Bass("TRN2", target_bir_lowering=False)

    x_hbm = nc.dram_tensor("x", [T, D], F32, kind="ExternalInput").ap()
    wq_h = nc.dram_tensor("Wq", [D, D], F32, kind="ExternalInput").ap()
    wk_h = nc.dram_tensor("Wk", [D, D], F32, kind="ExternalInput").ap()
    wv_h = nc.dram_tensor("Wv", [D, D], F32, kind="ExternalInput").ap()
    wo_h = nc.dram_tensor("Wo", [D, D], F32, kind="ExternalInput").ap()
    bq_h = nc.dram_tensor("bq", [D], F32, kind="ExternalInput").ap()
    bk_h = nc.dram_tensor("bk", [D], F32, kind="ExternalInput").ap()
    bv_h = nc.dram_tensor("bv", [D], F32, kind="ExternalInput").ap()
    bo_h = nc.dram_tensor("bo", [D], F32, kind="ExternalInput").ap()
    wfc_h = nc.dram_tensor("Wfc", [F, D], F32, kind="ExternalInput").ap()
    wpr_h = nc.dram_tensor("Wpr", [D, F], F32, kind="ExternalInput").ap()
    out_hbm = nc.dram_tensor("out", [T, D], F32, kind="ExternalOutput").ap()
    w2d_h = nc.dram_tensor("w2scratch", [32 * 128 * 1024], BF16,
                           kind="Internal").ap()
    w1d_h = nc.dram_tensor("w1scratch", [32 * 128 * 1024], BF16,
                           kind="Internal").ap()

    ctx = ExitStack()
    e = ctx.enter_context

    x_sb = e(nc.sbuf_tensor("x_sb", [128, T * D // 128], F32))
    regB = ExitStack()
    wq_sb = regB.enter_context(nc.sbuf_tensor("wq_sb", [128, 8192], BF16))
    wk_sb = regB.enter_context(nc.sbuf_tensor("wk_sb", [128, 8192], BF16))
    wv_sb = regB.enter_context(nc.sbuf_tensor("wv_sb", [128, 8192], BF16))
    wo_sb = regB.enter_context(nc.sbuf_tensor("wo_sb", [128, 8192], BF16))
    regC = ExitStack()
    xn1_bf = regC.enter_context(nc.sbuf_tensor("xn1_bf", [128, 4096], BF16))
    xn1t_sb = regC.enter_context(nc.sbuf_tensor("xn1t_sb", [128, 4096], BF16))
    q_bf = regC.enter_context(nc.sbuf_tensor("q_bf", [128, 4096], BF16))
    k_bf = regC.enter_context(nc.sbuf_tensor("k_bf", [128, 4096], BF16))
    v_bf = regC.enter_context(nc.sbuf_tensor("v_bf", [128, 4096], BF16))
    attn_sb = regC.enter_context(nc.sbuf_tensor("attn_sb", [128, 4096], BF16))
    s_f32 = [e(nc.sbuf_tensor(f"s_f32_{i}", [128, 256], F32, side="right"))
             for i in range(3)]
    p_bf = [e(nc.sbuf_tensor(f"p_bf_{i}", [128, 256], BF16, side="right"))
            for i in range(3)]
    pt_bf = [e(nc.sbuf_tensor(f"pt_bf_{i}", [128, 256], BF16, side="right"))
             for i in range(3)]
    ident = e(nc.sbuf_tensor("ident", [128, 128], BF16, side="right"))
    wtmp2 = [e(nc.sbuf_tensor(f"wtmp2_{i}", [128, 2048], BF16, side="right"))
             for i in range(2)]
    wtmp1 = [e(nc.sbuf_tensor(f"wtmp1_{i}", [128, 1024], BF16, side="right"))
             for i in range(4)]
    w2stage = [e(nc.sbuf_tensor(f"w2stage_{i}", [128, 1024], BF16,
                                side="right")) for i in range(2)]
    bo_bf = e(nc.sbuf_tensor("bo_bf", [1, 1024], BF16, side="right"))
    ones_bf = e(nc.sbuf_tensor("ones_bf", [1, 128], BF16, side="right"))
    bq_sb = e(nc.sbuf_tensor("bq_sb", [128, 8], F32, side="right"))
    bk_sb = e(nc.sbuf_tensor("bk_sb", [128, 8], F32, side="right"))
    bv_sb = e(nc.sbuf_tensor("bv_sb", [128, 8], F32, side="right"))
    bqs_sb = e(nc.sbuf_tensor("bqs_sb", [128, 8], F32, side="right"))
    sums = e(nc.sbuf_tensor("sums", [128, 4], F32, side="right"))
    sqsums = e(nc.sbuf_tensor("sqsums", [128, 4], F32, side="right"))
    mmean = e(nc.sbuf_tensor("mmean", [128, 4], F32, side="right"))
    nmean = e(nc.sbuf_tensor("nmean", [128, 4], F32, side="right"))
    vtmp = e(nc.sbuf_tensor("vtmp", [128, 4], F32, side="right"))
    veps = e(nc.sbuf_tensor("veps", [128, 4], F32, side="right"))
    rstd = e(nc.sbuf_tensor("rstd", [128, 4], F32, side="right"))
    negmax = e(nc.sbuf_tensor("negmax", [128, 4], F32, side="right"))
    ssum = e(nc.sbuf_tensor("ssum", [128, 12], F32, side="right"))
    rsum = e(nc.sbuf_tensor("rsum", [128, 12], F32, side="right"))

    psA = e(nc.psum_tensor("psA", [128, 1024], F32))
    psB = e(nc.psum_tensor("psB", [128, 1024], F32))
    psC = e(nc.psum_tensor("psC", [128, 1024], F32))
    psD = e(nc.psum_tensor("psD", [128, 2048], BF16))

    S = {nm: e(nc.semaphore(nm)) for nm in ("pe", "ve", "se", "gp")}
    NSY, NGP, NSE = 6, 10, 6
    for i in range(NSY):
        S[f"qs{i}"] = e(nc.semaphore(f"qs{i}"))
    for i in range(NGP):
        S[f"qg{i}"] = e(nc.semaphore(f"qg{i}"))
    for i in range(NSE):
        S[f"qe{i}"] = e(nc.semaphore(f"qe{i}"))

    plan = {k: [] for k in ("pe", "ve", "se", "gp", "sy")}
    n = dict(pe=0, ve=0, se=0, gp=0)
    for i in range(NSY):
        n[f"qs{i}"] = 0
    for i in range(NGP):
        n[f"qg{i}"] = 0
    for i in range(NSE):
        n[f"qe{i}"] = 0

    def emit(eng, fn):
        plan[eng].append(fn)

    def wait(eng, sem, val):
        if val <= 0:
            return
        assert val <= n[sem], (eng, sem, val, n[sem])
        emit(eng, lambda E, sem=sem, val=val: E.wait_ge(S[sem], val))

    def op(eng, build, inc=None):
        if inc is None:
            emit(eng, lambda E, b=build: b(E))
            return None
        n[inc] += 1
        emit(eng, lambda E, b=build, s=inc: b(E).then_inc(S[s], 1))
        return n[inc]

    qidx = {"sy": 0, "gp": 0, "se": 0}

    def dma(eng, build):
        npool = {"sy": NSY, "gp": NGP, "se": NSE}[eng]
        pfx = {"sy": "qs", "gp": "qg", "se": "qe"}[eng]
        k = qidx[eng] % npool
        qidx[eng] += 1
        nm = f"{pfx}{k}"
        if n[nm] > 0:
            wait(eng, nm, n[nm])
        n[nm] += 16
        emit(eng, lambda E, b=build, nm=nm: b(E).then_inc(S[nm], 16))
        return (nm, n[nm])

    op("gp", lambda E: E.iota(ident[:], [[1, 128]], base=0,
                              channel_multiplier=-1,
                              allow_small_or_imprecise_dtypes=True), inc="gp")
    wait("gp", "gp", n["gp"])
    op("gp", lambda E: E.tensor_single_scalar(ident[:], ident[:], 0.0,
                                              ALU.is_equal), inc="gp")
    E_ident = n["gp"]
    op("gp", lambda E: E.memset(ones_bf[:], 1.0), inc="gp")
    E_ones = n["gp"]

    E_xp = []
    E_xp.append(dma("sy", lambda E: E.dma_start(
        out=x_sb[:, 0:4096].rearrange("p (j d) -> p j d", d=D),
        in_=x_hbm[0:TPA, :].rearrange("(j p) d -> p j d", p=128))))
    E_bias = {}
    for nm2, bh, bsb in (("q", bq_h, bq_sb), ("k", bk_h, bk_sb),
                         ("v", bv_h, bv_sb)):
        E_bias[nm2] = dma("sy", lambda E, bh=bh, bsb=bsb: E.dma_start(
            out=bsb[:], in_=bass.AP(bh.tensor, 0, [[1, 128], [128, 8]]),
            allow_slow_non_contiguous=True))
    for p in range(1, PA):
        E_xp.append(dma("sy", lambda E, p=p: E.dma_start(
            out=x_sb[:, p * 4096:(p + 1) * 4096].rearrange(
                "p (j d) -> p j d", d=D),
            in_=x_hbm[p * TPA:(p + 1) * TPA, :].rearrange(
                "(j p) d -> p j d", p=128))))

    E_w = {}
    E_wq_half = dma("gp", lambda E: E.dma_start(
        out=bass.AP(wq_sb[:].tensor, 0, [[8192, 128], [1024, 8], [1, 512]]),
        in_=wq_h[:, 0:512].rearrange("(c p) d -> p c d", p=128)))
    E_w["q"] = dma("gp", lambda E: E.dma_start(
        out=bass.AP(wq_sb[:].tensor, 512, [[8192, 128], [1024, 8], [1, 512]]),
        in_=wq_h[:, 512:1024].rearrange("(c p) d -> p c d", p=128)))
    for nm, wh, wsb in (("k", wk_h, wk_sb), ("v", wv_h, wv_sb),
                        ("o", wo_h, wo_sb)):
        E_w[nm] = dma("gp", lambda E, wh=wh, wsb=wsb: E.dma_start(
            out=wsb[:].rearrange("p (c d) -> p c d", d=D),
            in_=wh.rearrange("(c p) d -> p c d", p=128)))
    E_bo = dma("gp", lambda E: E.dma_start(
        out=bo_bf[:], in_=bass.AP(bo_h.tensor, 0, [[0, 1], [1, 1024]])))

    chunks2 = [(m, fh) for m in range(8) for fh in range(2)]
    E_ld2 = {}
    E_tr2 = {}
    E_st2 = {}
    w2g = {"gidx": 0}

    def w2_dma(ci):
        m, fh = chunks2[ci]
        if ci >= 2:
            wait("gp", "pe", E_tr2[ci - 2])
        E_ld2[ci] = dma("gp", lambda E, m=m, fh=fh, ci=ci: E.dma_start(
            out=wtmp2[ci % 2][:],
            in_=wpr_h[m * 128:(m + 1) * 128, fh * 2048:(fh + 1) * 2048]))


    w2_dma(0)
    w2_dma(1)

    wait("ve", *E_bias["q"])
    wait("ve", *E_bias["k"])
    wait("ve", *E_bias["v"])
    op("ve", lambda E: E.tensor_single_scalar(bqs_sb[:], bq_sb[:], DKS,
                                              ALU.mult), inc="ve")
    E_bqs = n["ve"]

    E_w1ld = {}
    E_w1tr = {}

    def w1_dma(fc):
        if fc >= 4:
            wait("gp", "pe", E_w1tr[fc - 4])
        E_w1ld[fc] = dma("gp", lambda E, fc=fc: E.dma_start(
            out=wtmp1[fc % 4][:], in_=wfc_h[fc * 128:(fc + 1) * 128, :]))

    for fc in range(4):
        w1_dma(fc)

    relQ = [("ve", 0)] * 4
    relD = [("se", 0), ("se", 0)]
    relC2 = [("ve", 0), ("ve", 0)]

    qkv_quarters = [psA[:, 0:512], psA[:, 512:1024],
                    psB[:, 0:512], psB[:, 512:1024]]

    def pe_wait_rel(rel):
        wait("pe", rel[0], rel[1])

    E_x2 = {}
    war = {"tta": 0, "p": 0}

    E_xn1_d = {}
    E_xn1t_fill_prev = [0]

    def ln1_part1(p, jj):
        xtj = x_sb[:, (4 * p + jj) * 1024:(4 * p + jj + 1) * 1024]
        xoj = xn1_bf[:, jj * 1024:(jj + 1) * 1024]
        if jj == 0:
            wait("se", *E_xp[p])
            wait("se", "pe", E_xn1t_fill_prev[0])
        op("se", lambda E, xtj=xtj, xoj=xoj: E.activation(
            xoj, xtj, ACT.Identity, accum_out=sums[:, jj:jj + 1]),
            inc="se")
        wait("se", "se", n["se"])
        op("se", lambda E, xtj=xtj, xoj=xoj: E.activation(
            xoj, xtj, ACT.Square, accum_out=sqsums[:, jj:jj + 1]),
            inc="se")

    def ln1_part2(p):
        wait("ve", "se", n["se"])
        for b_ in [
            lambda E: E.tensor_single_scalar(
                nmean[:, :4], sums[:, :4], -1.0 / D, ALU.mult),
            lambda E: E.tensor_single_scalar(
                mmean[:, :4], sums[:, :4], 1.0 / D, ALU.mult),
            lambda E: E.tensor_mul(mmean[:, :4], mmean[:, :4], mmean[:, :4]),
            lambda E: E.scalar_tensor_tensor(
                veps[:, :4], sqsums[:, :4], 1.0 / D, mmean[:, :4],
                ALU.mult, ALU.subtract),
            lambda E: E.tensor_single_scalar(
                veps[:, :4], veps[:, :4], EPS, ALU.add),
        ]:
            op("ve", b_, inc="ve")
            wait("ve", "ve", n["ve"])
        wait("se", "ve", n["ve"])
        op("se", lambda E: E.activation(vtmp[:, :4], veps[:, :4], ACT.Ln),
           inc="se")
        wait("se", "se", n["se"])
        op("se", lambda E: E.activation(rstd[:, :4], vtmp[:, :4], ACT.Exp,
                                        scale=-0.5), inc="se")

    def ln1_part3(p):
        wait("ve", "se", n["se"])
        wait("ve", "pe", E_xn1t_fill_prev[0])
        for jj in range(4):
            xtj = x_sb[:, (4 * p + jj) * 1024:(4 * p + jj + 1) * 1024]
            xoj = xn1_bf[:, jj * 1024:(jj + 1) * 1024]
            op("ve", lambda E, jj=jj, xtj=xtj, xoj=xoj: E.tensor_scalar(
                xoj, xtj, nmean[:, jj:jj + 1], rstd[:, jj:jj + 1],
                ALU.add, ALU.mult), inc="ve" if jj == 3 else None)
        E_xn1_d[p] = n["ve"]

    for jj in range(4):
        ln1_part1(0, jj)
    ln1_part2(0)
    ln1_part3(0)

    E_xn1t_d = {}

    def xn1t_group(p, jj):
        if jj == 0:
            wait("pe", "ve", E_xn1_d[p])
            if p == 0:
                wait("pe", "gp", E_ident)
        pe_wait_rel(relD[jj % 2])
        for c in range(8):
            op("pe", lambda E, jj=jj, c=c: E.transpose(
                psD[:, (jj % 2) * 1024 + c * 128:
                    (jj % 2) * 1024 + (c + 1) * 128],
                xn1_bf[:, jj * 1024 + c * 128:jj * 1024 + (c + 1) * 128],
                ident[:]), inc="pe" if c == 7 else None)
        fill = n["pe"]
        wait("se", "pe", fill)
        dst = bass.AP(xn1t_sb[:].tensor, jj * 128,
                      [[4096, 128], [512, 8], [1, 128]])
        src = psD[:, (jj % 2) * 1024:(jj % 2) * 1024 + 1024].rearrange(
            "p (c t) -> p c t", c=8)
        op("se", lambda E, dst=dst, src=src: E.copy(dst, src),
           inc="se")
        relD[jj % 2] = ("se", n["se"])
        E_xn1t_d[(p, jj)] = n["se"]
        if jj == 3:
            E_xn1t_fill_prev[0] = fill

    for p in range(PA):
        if p == 0:
            for jj in range(4):
                xn1t_group(0, jj)

        E_qe, E_ke = [0] * 8, [0] * 8
        wait("pe", "se", E_xn1t_d[(p, 3)])
        if p == 0:
            wait("pe", *E_wq_half)
        g = 0
        for oc in range(8):
            if p == 0 and oc == 4:
                wait("pe", *E_w["q"])
            qb = qkv_quarters[g % 4]
            pe_wait_rel(relQ[g % 4])
            for c in range(8):
                op("pe", lambda E, oc=oc, c=c, qb=qb: E.matmul(
                    qb, wq_sb[:, c * 1024 + oc * 128:c * 1024 + (oc + 1) * 128],
                    xn1t_sb[:, c * 512:(c + 1) * 512],
                    start=(c == 0), stop=(c == 7)),
                    inc="pe" if c == 7 else None)
            wait("se", "pe", n["pe"])
            if p == 0 and oc == 0:
                wait("se", "ve", E_bqs)
            op("se", lambda E, oc=oc, qb=qb: E.activation(
                q_bf[:, oc * 512:(oc + 1) * 512], qb, ACT.Identity,
                bias=bqs_sb[:, oc:oc + 1], scale=DKS), inc="se")
            relQ[g % 4] = ("se", n["se"])
            E_qe[oc] = n["se"]
            g += 1
        if p == 0:
            wait("pe", *E_w["k"])
        for oc in range(8):
            qb = qkv_quarters[g % 4]
            pe_wait_rel(relQ[g % 4])
            for c in range(8):
                op("pe", lambda E, oc=oc, c=c, qb=qb: E.matmul(
                    qb, wk_sb[:, c * 1024 + oc * 128:c * 1024 + (oc + 1) * 128],
                    xn1t_sb[:, c * 512:(c + 1) * 512],
                    start=(c == 0), stop=(c == 7)),
                    inc="pe" if c == 7 else None)
            wait("se", "pe", n["pe"])
            op("se", lambda E, oc=oc, qb=qb: E.activation(
                k_bf[:, oc * 512:(oc + 1) * 512], qb, ACT.Identity,
                bias=bk_sb[:, oc:oc + 1]), inc="se")
            relQ[g % 4] = ("se", n["se"])
            E_ke[oc] = n["se"]
            g += 1
        if p == 0:
            wait("pe", *E_w["v"])
        for jj in range(4):
            for half in range(2):
                qb = qkv_quarters[g % 4]
                pe_wait_rel(relQ[g % 4])
                for c in range(8):
                    op("pe", lambda E, jj=jj, half=half, c=c, qb=qb: E.matmul(
                        qb,
                        xn1t_sb[:, c * 512 + jj * 128:c * 512 + (jj + 1) * 128],
                        wv_sb[:, c * 1024 + half * 512:
                              c * 1024 + (half + 1) * 512],
                        start=(c == 0), stop=(c == 7)),
                        inc="pe" if c == 7 else None)
                wait("ve", "pe", n["pe"])
                op("ve", lambda E, jj=jj, half=half, qb=qb: E.tensor_copy(
                    v_bf[:, jj * 1024 + half * 512:
                         jj * 1024 + (half + 1) * 512], qb), inc="ve")
                relQ[g % 4] = ("ve", n["ve"])
                g += 1
        E_v_all = n["ve"]

        E_exp4 = [0] * 8
        E_p = [0] * 8
        E_ptev = [0] * 8
        E_pt = [0] * 8
        E_pv = [0] * 8
        E_attnev = [0] * 8

        def scores(h):
            wait("pe", "se", max(E_qe[h], E_ke[h]))
            if h >= 2:
                wait("pe", "se", E_exp4[h - 2])
            else:
                pe_wait_rel(relC2[h % 2])
            for bl in range(8):
                op("pe", lambda E, h=h, bl=bl: E.matmul(
                    psC[(bl // 4) * 64:(bl // 4) * 64 + 64,
                        (h % 2) * 512 + (bl % 4) * 64:
                        (h % 2) * 512 + (bl % 4) * 64 + 64],
                    q_bf[:, h * 512 + bl * 64:h * 512 + (bl + 1) * 64],
                    k_bf[:, h * 512 + bl * 64:h * 512 + (bl + 1) * 64],
                    start=True, stop=True), inc="pe" if bl == 7 else None)
            return n["pe"]

        def exp4_h(h, E_s):
            wait("se", "pe", E_s)
            c0 = (h % 3) * 4
            for g in range(4):
                op("se", lambda E, h=h, g=g, c0=c0: E.activation(
                    s_f32[h % 3][:, g * 64:(g + 1) * 64],
                    psC[:, (h % 2) * 512 + g * 64:
                        (h % 2) * 512 + (g + 1) * 64],
                    ACT.Exp, accum_out=ssum[:, c0 + g:c0 + g + 1]),
                    inc="se" if g == 3 else None)
            E_exp4[h] = n["se"]

        def softmax_post(h):
            c0 = (h % 3) * 4
            wait("ve", "se", E_exp4[h])
            if h >= 3:
                wait("ve", "pe", E_pt[h - 3])
            op("ve", lambda E, c0=c0: E.reciprocal(
                rsum[:, c0:c0 + 4], ssum[:, c0:c0 + 4]), inc="ve")
            wait("ve", "ve", n["ve"])
            g3 = s_f32[h % 3][:].rearrange("p (g k) -> p g k", g=4)
            op("ve", lambda E, h=h, g3=g3, c0=c0: E.tensor_tensor(
                p_bf[h % 3][:].rearrange("p (g k) -> p g k", g=4), g3,
                rsum[:, c0:c0 + 4].unsqueeze(2).broadcast_to([128, 4, 64]),
                ALU.mult), inc="ve")
            E_p[h] = n["ve"]

        def pt_h(h):
            wait("pe", "ve", E_p[h])
            pe_wait_rel(relD[h % 2])
            base = (h % 2) * 1024
            for t_ in range(2):
                op("pe", lambda E, h=h, t_=t_, base=base: E.transpose(
                    psD[:, base + t_ * 128:base + (t_ + 1) * 128],
                    p_bf[h % 3][:, t_ * 128:(t_ + 1) * 128], ident[:]),
                    inc="pe" if t_ == 1 else None)
            E_pt[h] = n["pe"]

        def ptev_h(h):
            wait("se", "pe", E_pt[h])
            base = (h % 2) * 1024
            op("se", lambda E, h=h, base=base: E.copy(
                pt_bf[h % 3][:], psD[:, base:base + 256]), inc="se")
            E_ptev[h] = n["se"]
            relD[h % 2] = ("se", n["se"])

        def pv_h(h):
            wait("pe", "se", E_ptev[h])
            if h < 2:
                wait("pe", "ve", E_v_all)
                wait("pe", "se", max(E_qe[7], E_ke[7]))
            else:
                wait("pe", "ve", E_attnev[h - 2])
            pair = psA if h % 2 == 0 else psB
            for bl in range(8):
                panel = bl % 4
                tt = panel // 2
                kb = (panel % 2) * 64
                qc = (bl // 4) * 64
                col = (bl % 2) * 512 + (bl // 2) * 64
                op("pe", lambda E, h=h, bl=bl, tt=tt, kb=kb, qc=qc, col=col,
                   pair=pair: E.matmul(
                       pair[:, col:col + 64],
                       v_bf[(bl % 2) * 64:(bl % 2) * 64 + 64,
                            (bl // 2) * 1024 + h * 128:
                            (bl // 2) * 1024 + (h + 1) * 128],
                       pt_bf[h % 3][kb:kb + 64,
                                    tt * 128 + qc:tt * 128 + qc + 64],
                       start=True, stop=True),
                    inc="pe" if bl == 7 else None)
            E_pv[h] = n["pe"]

        def attnev_h(h):
            wait("ve", "pe", E_pv[h])
            pair = psA if h % 2 == 0 else psB
            src = bass.AP(pair[:].tensor, 0,
                          [[1024, 128], [512, 2], [64, 4], [1, 64]])
            dst = bass.AP(attn_sb[:].tensor, h * 512,
                          [[4096, 128], [64, 2], [128, 4], [1, 64]])
            op("ve", lambda E, h=h, src=src, dst=dst: E.tensor_scalar(
                dst, src, bv_sb[:, h:h + 1], None, ALU.add), inc="ve")
            E_attnev[h] = n["ve"]

        def w1_group(fc):
            half = w2g["gidx"] % 2
            pe_wait_rel(relD[half])
            for c in range(8):
                op("pe", lambda E, fc=fc, c=c, half=half: E.transpose(
                    psD[:, half * 1024 + c * 128:half * 1024 + (c + 1) * 128],
                    wtmp1[fc % 4][:, c * 128:(c + 1) * 128], ident[:]),
                    inc="pe" if c == 7 else None)
            fill = n["pe"]
            E_w1tr[fc] = n["pe"]
            if fc + 4 < 32:
                w1_dma(fc + 4)
            wait("se", "pe", fill)
            if w2g["gidx"] >= 2:
                wait("se", *E_st2[w2g["gidx"] - 2])
            op("se", lambda E, half=half, s=w2g["gidx"] % 2: E.copy(
                w2stage[s][:], psD[:, half * 1024:half * 1024 + 1024]),
                inc="se")
            relD[half] = ("se", n["se"])
            wait("se", "se", n["se"])
            dst = bass.AP(w1d_h.tensor, fc * 131072, [[1024, 128], [1, 1024]])
            E_st2[w2g["gidx"]] = dma("se", lambda E, dst=dst,
                                     s=w2g["gidx"] % 2:
                                     E.dma_start(out=dst, in_=w2stage[s][:]))
            w2g["gidx"] += 1

        def w2_group(ci, grp):
            half = w2g["gidx"] % 2
            pe_wait_rel(relD[half])
            for t_ in range(8):
                tl = grp * 8 + t_
                op("pe", lambda E, ci=ci, tl=tl, half=half: E.transpose(
                    psD[:, half * 1024 + (tl % 8) * 128:
                        half * 1024 + (tl % 8 + 1) * 128],
                    wtmp2[ci % 2][:, tl * 128:(tl + 1) * 128], ident[:]),
                    inc="pe" if t_ == 7 else None)
            fill = n["pe"]
            m, fh = chunks2[ci]
            wait("ve", "pe", fill)
            if w2g["gidx"] >= 2:
                wait("ve", *E_st2[w2g["gidx"] - 2])
            op("ve", lambda E, half=half, s=w2g["gidx"] % 2: E.tensor_copy(
                w2stage[s][:], psD[:, half * 1024:half * 1024 + 1024]),
                inc="ve")
            relD[half] = ("ve", n["ve"])
            base = (fh * 16 + grp * 8) * 131072 + m * 128
            dst = bass.AP(w2d_h.tensor, base,
                          [[1024, 128], [131072, 8], [1, 128]])
            wait("sy", "ve", n["ve"])
            E_st2[w2g["gidx"]] = dma("sy", lambda E, dst=dst,
                                     s=w2g["gidx"] % 2:
                                     E.dma_start(out=dst, in_=w2stage[s][:]))
            w2g["gidx"] += 1
            if grp == 1:
                E_tr2[ci] = n["pe"]
                if ci + 2 < len(chunks2):
                    w2_dma(ci + 2)

        E_s = scores(0)
        exp4_h(0, E_s)
        E_s = scores(1)
        exp4_h(1, E_s)
        softmax_post(0)
        pt_h(0)
        ptev_h(0)
        E_s = scores(2)
        exp4_h(2, E_s)
        softmax_post(1)
        pt_h(1)
        ptev_h(1)
        for h in range(8):
            if h + 2 < 8:
                softmax_post(h + 2)
                pt_h(h + 2)
                ptev_h(h + 2)
            if h + 3 < 8:
                E_s = scores(h + 3)
                exp4_h(h + 3, E_s)
            pv_h(h)
            attnev_h(h)
            ci = 4 * p + h // 2
            if h % 2 == 0:
                wait("pe", *E_ld2[ci])
            w2_group(ci, h % 2)
            fcw = 8 * p + h
            wait("pe", *E_w1ld[fcw])
            w1_group(fcw)
            if p + 1 < PA:
                if 1 <= h <= 4:
                    ln1_part1(p + 1, h - 1)
                elif h == 5:
                    ln1_part2(p + 1)
                elif h == 6:
                    ln1_part3(p + 1)
        relC2[0] = ("se", E_exp4[6])
        relC2[1] = ("se", E_exp4[7])

        if p == 0:
            wait("pe", *E_bo)
            wait("pe", *E_w["o"])
            wait("pe", "gp", E_ones)
        wait("pe", "ve", E_attnev[7])
        for jj in range(4):
            pair = psA if jj % 2 == 0 else psB
            if jj >= 2:
                wait("pe", "ve", E_x2[4 * p + jj - 2])
            for half in range(2):
                for c in range(8):
                    op("pe", lambda E, jj=jj, half=half, c=c, pair=pair:
                       E.matmul(
                           pair[:, half * 512:(half + 1) * 512],
                           attn_sb[:, c * 512 + jj * 128:
                                   c * 512 + (jj + 1) * 128],
                           wo_sb[:, c * 1024 + half * 512:
                                 c * 1024 + (half + 1) * 512],
                           start=(c == 0), stop=False))
                op("pe", lambda E, half=half, pair=pair: E.matmul(
                    pair[:, half * 512:(half + 1) * 512], ones_bf[:],
                    bo_bf[0:1, half * 512:(half + 1) * 512],
                    start=False, stop=True), inc="pe" if half == 1 else None)
            E_opj = n["pe"]
            if p + 1 < PA:
                xn1t_group(p + 1, jj)
            wait("ve", "pe", E_opj)
            jg = 4 * p + jj
            for half in range(2):
                xs = x_sb[:, jg * 1024 + half * 512:jg * 1024 + (half + 1) * 512]
                op("ve", lambda E, xs=xs, pair=pair, half=half:
                   E.scalar_tensor_tensor(
                       xs, pair[:, half * 512:(half + 1) * 512], 1.0, xs,
                       ALU.mult, ALU.add), inc="ve" if half == 1 else None)
            E_x2[jg] = n["ve"]
        relQ[0] = relQ[1] = ("ve", E_x2[4 * p + 2])
        relQ[2] = relQ[3] = ("ve", E_x2[4 * p + 3])

    E_x2_all = n["ve"]
    E_attn_end_pe = n["pe"]

    regC.close()
    regB.close()
    E_w2_all = [(f"qe{i}", n[f"qe{i}"]) for i in range(NSE)]

    xn2_bf = [e(nc.sbuf_tensor(f"xn2_bf_{i}", [128, 1024], BF16))
              for i in range(2)]
    xn2t_sb = e(nc.sbuf_tensor("xn2t_sb", [128, 2048], BF16))
    h_bf = [e(nc.sbuf_tensor(f"h_bf_{i}", [128, 256], BF16)) for i in range(2)]
    w2slab = [e(nc.sbuf_tensor(f"w2slab_{i}", [128, 2048], BF16))
              for i in range(4)]
    w1slab = [e(nc.sbuf_tensor(f"w1slab_{i}", [128, 2048], BF16))
              for i in range(4)]
    y_sb = e(nc.sbuf_tensor("y_sb", [128, 2048], F32))
    mem_q = e(nc.sbuf_tensor("mem_q", [128, 2048], F32))
    spk_q = e(nc.sbuf_tensor("spk_q", [128, 2048], BF16))
    acc_q = e(nc.sbuf_tensor("acc_q", [128, 2048], BF16))

    relH = relC2
    relD6 = [relD[0]]
    relD7 = [relD[1]]
    E_slab1 = {}

    def slab1_load(p, pr, prev_fc):
        if p == 0 and pr == 0:
            for sm, vv in E_w2_all:
                wait("sy", sm, vv)
        if prev_fc:
            wait("sy", "pe", prev_fc)
        ev = dma("sy", lambda E, pr=pr: E.dma_start(
            out=w1slab[pr % 4][:].rearrange("p (f c) -> p f c", f=2),
            in_=bass.AP(w1d_h.tensor, pr * 262144,
                        [[1024, 128], [131072, 2], [1, 1024]])))
        E_slab1[(p, 2 * pr)] = ev
        E_slab1[(p, 2 * pr + 1)] = ev

    E_yev = {}
    E_spk_yfree = {}
    E_spk_vdone = {}
    E_spk_gdone = {}
    E_final = {}
    E_accT = {}
    E_w1ld = {}
    E_w1ev = {}
    E_xn2t = {}
    E_norm2_d = {}
    E_proj_pe = {}
    E_slab = {}
    E_xn2t_fill = [0, 0]

    def ln2_tile(p, jj):
        jg = 2 * p + jj
        xs = x_sb[:, jg * 1024:(jg + 1) * 1024]
        if p == 0 and jj == 0:
            wait("ve", "ve", E_x2_all)
            wait("se", "ve", E_x2_all)
        op("ve", lambda E, xs=xs: E.reduce_sum(sums[:, 0:1], xs, AX.X),
           inc="ve")
        E_red = n["ve"]
        wait("se", "pe", E_xn2t_fill[jj])
        op("se", lambda E, xs=xs, jj=jj: E.activation(
            xn2_bf[jj][:], xs, ACT.Square, accum_out=sqsums[:, 0:1]),
            inc="se")
        E_sq2 = n["se"]
        wait("ve", "ve", E_red)
        wait("ve", "se", E_sq2)
        for b_ in [
            lambda E: E.tensor_single_scalar(
                nmean[:, 0:1], sums[:, 0:1], -1.0 / D, ALU.mult),
            lambda E: E.tensor_single_scalar(
                mmean[:, 0:1], sums[:, 0:1], 1.0 / D, ALU.mult),
            lambda E: E.tensor_mul(mmean[:, 0:1], mmean[:, 0:1],
                                   mmean[:, 0:1]),
            lambda E: E.scalar_tensor_tensor(
                veps[:, 0:1], sqsums[:, 0:1], 1.0 / D, mmean[:, 0:1],
                ALU.mult, ALU.subtract),
            lambda E: E.tensor_single_scalar(
                veps[:, 0:1], veps[:, 0:1], EPS, ALU.add),
        ]:
            op("ve", b_, inc="ve")
            wait("ve", "ve", n["ve"])
        E_veps2 = n["ve"]
        wait("se", "ve", E_veps2)
        op("se", lambda E: E.activation(vtmp[:, 0:1], veps[:, 0:1],
                                        ACT.Ln), inc="se")
        wait("se", "se", n["se"])
        op("se", lambda E: E.activation(rstd[:, 0:1], vtmp[:, 0:1],
                                        ACT.Exp, scale=-0.5), inc="se")
        E_rstd2 = n["se"]
        wait("ve", "se", E_rstd2)
        op("ve", lambda E, xs=xs, jj=jj: E.tensor_scalar(
            xn2_bf[jj][:], xs, nmean[:, 0:1], rstd[:, 0:1],
            ALU.add, ALU.mult), inc="ve")
        E_norm2_d[(p, jj)] = n["ve"]

    def ln2_transposes(p, jj):
        wait("pe", "ve", E_norm2_d[(p, jj)])
        pe_wait_rel(relD6[0])
        for c in range(8):
            op("pe", lambda E, jj=jj, c=c: E.transpose(
                psD[:, c * 128:(c + 1) * 128],
                xn2_bf[jj][:, c * 128:(c + 1) * 128], ident[:]),
                inc="pe" if c == 7 else None)
        E_xn2t_fill[jj] = n["pe"]
        wait("se", "pe", E_xn2t_fill[jj])
        dst = bass.AP(xn2t_sb[:].tensor, jj * 128,
                      [[2048, 128], [256, 8], [1, 128]])
        src2 = psD[:, 0:1024].rearrange("p (c t) -> p c t", c=8)
        op("se", lambda E, dst=dst, src2=src2: E.copy(dst, src2),
           inc="se")
        relD6[0] = ("se", n["se"])
        E_xn2t[p] = n["se"]

    def slab_load(p, pr, prev_proj):
        if p == 0 and pr == 0:
            for sm, vv in E_w2_all:
                wait("sy", sm, vv)
        if prev_proj:
            wait("sy", "pe", prev_proj)
        ev = dma("sy", lambda E, pr=pr: E.dma_start(
            out=w2slab[pr % 4][:].rearrange("p (f c) -> p f c", f=2),
            in_=bass.AP(w2d_h.tensor, pr * 262144,
                        [[1024, 128], [131072, 2], [1, 1024]])))
        E_slab[(p, 2 * pr)] = ev
        E_slab[(p, 2 * pr + 1)] = ev

    ln2_tile(0, 0)
    ln2_tile(0, 1)
    for pr in range(4):
        slab_load(0, pr, 0)
        slab1_load(0, pr, 0)

    def emit_accT(pp, gd=None):
        for jj in range(2):
            wait("pe", "gp", E_spk_gdone[pp] if gd is None else gd[jj])
            prev = E_final.get((pp, jj - 1), E_final.get((pp - 1, 1), 0))
            wait("pe", "ve", prev)
            for c in range(8):
                op("pe", lambda E, jj=jj, c=c: E.transpose(
                    psD[:, 1024 + c * 128:1024 + (c + 1) * 128],
                    acc_q[:, c * 256 + jj * 128:c * 256 + (jj + 1) * 128],
                    ident[:]), inc="pe" if c == 7 else None)
            fill = n["pe"]
            jg = 2 * pp + jj
            xs = x_sb[:, jg * 1024:(jg + 1) * 1024]
            wait("ve", "pe", fill)
            op("ve", lambda E, xs=xs: E.scalar_tensor_tensor(
                xs, psD[:, 1024:2048], 0.2, xs, ALU.mult, ALU.add), inc="ve")
            E_final[(pp, jj)] = n["ve"]
            wait("gp", "ve", n["ve"])
            dma("gp", lambda E, xs=xs, jg=jg: E.dma_start(
                out=out_hbm[jg * 128:(jg + 1) * 128, :], in_=xs))
        E_accT[pp] = n["pe"]

    for p in range(PF):
        E_sqf = {}
        ln2_transposes(p, 0)
        ln2_transposes(p, 1)
        wait("pe", "se", E_xn2t[p])
        for step in range(33):
            fc = step
            if fc < 32:
                eng, cnt = relH[fc % 2]
                if cnt > 0:
                    wait("pe", eng, cnt)
                wait("pe", *E_slab1[(p, fc)])
                hp = psC[:, (fc % 2) * 512:(fc % 2) * 512 + 256]
                s1b = w1slab[(fc // 2) % 4][:, (fc % 2) * 1024:
                                            (fc % 2) * 1024 + 1024]
                for c in range(8):
                    op("pe", lambda E, fc=fc, c=c, hp=hp, s1b=s1b: E.matmul(
                        hp, s1b[:, c * 128:(c + 1) * 128],
                        xn2t_sb[:, c * 256:(c + 1) * 256],
                        start=(c == 0), stop=(c == 7)),
                        inc="pe" if c == 7 else None)
                E_fcf = n["pe"]
                if fc % 2 == 1 and fc // 2 + 4 < 16:
                    slab1_load(p, fc // 2 + 4, E_fcf)
                if p + 1 < PF and fc in (24, 28):
                    ln2_tile(p + 1, (fc - 24) // 4)
                wait("se", "pe", E_fcf)
                op("se", lambda E, fc=fc, hp=hp: E.activation(
                    h_bf[fc % 2][:], hp, ACT.Relu), inc="se")
                wait("se", "se", n["se"])
                op("se", lambda E, fc=fc: E.activation(
                    h_bf[fc % 2][:], h_bf[fc % 2][:], ACT.Square), inc="se")
                E_sqf[fc] = n["se"]
                relH[fc % 2] = ("se", E_sqf[fc])
            if step >= 1:
                fp = step - 1
                wait("pe", "se", E_sqf[fp])
                wait("pe", *E_slab[(p, fp)])
                if fp == 0:
                    if p > 0:
                        for q in range(4):
                            wait("pe", "se", E_yev[(p - 1, q)])
                    else:
                        wait("pe", "ve", E_x2_all)
                slb = w2slab[(fp // 2) % 4][:, (fp % 2) * 1024:
                                            (fp % 2) * 1024 + 1024]
                for m in range(8):
                    tgt = psA if m < 4 else psB
                    mc = m % 4
                    op("pe", lambda E, fp=fp, m=m, tgt=tgt, mc=mc, slb=slb:
                       E.matmul(
                           tgt[:, mc * 256:(mc + 1) * 256],
                           slb[:, m * 128:(m + 1) * 128],
                           h_bf[fp % 2][:],
                           start=(fp == 0 and m % 2 == 0),
                           stop=(fp == 31 and m % 2 == 1)),
                       inc="pe" if m == 7 else None)
                E_proj_pe[(p, fp)] = n["pe"]
                if fp % 2 == 1 and fp // 2 + 4 < 16:
                    slab_load(p, fp // 2 + 4, E_proj_pe[(p, fp)])
        E_y_fill = n["pe"]
        if p + 1 < PF:
            for j in range(4):
                slab_load(p + 1, j, E_proj_pe[(p, min(25 + 2 * j, 31))])
                slab1_load(p + 1, j, E_y_fill)
        if p > 0:
            emit_accT(p - 1)

        for q in range(4):
            ps = psA if q < 2 else psB
            half = q % 2
            wait("se", "pe", E_y_fill)
            if (p - 1) in E_spk_yfree:
                wait("se", "ve", E_spk_yfree[p - 1])
            op("se", lambda E, ps=ps, half=half, q=q: E.copy(
                y_sb[:, q * 512:(q + 1) * 512],
                ps[:, half * 512:(half + 1) * 512]), inc="se")
            E_yev[(p, q)] = n["se"]

        wait("ve", "se", E_yev[(p, 3)])
        if (p - 1) in E_spk_vdone:
            wait("ve", "ve", E_spk_vdone[p - 1])
        if (p - 1) in E_spk_gdone:
            wait("ve", "gp", E_spk_gdone[p - 1])

        def vop(build):
            op("ve", build, inc="ve")
            wait("ve", "ve", n["ve"])

        nhalf = 2 if p == PF - 1 else 1
        E_spk_gd_half = []

        def ap3(t, hf):
            return bass.AP(t[:].tensor, hf * 128,
                           [[2048, 128], [256, 8], [1, 128]])

        for hf in range(nhalf):
            if nhalf == 1:
                yv, sv, mv, av = y_sb[:], spk_q[:], mem_q[:], acc_q[:]
            else:
                yv, sv, mv, av = (ap3(y_sb, hf), ap3(spk_q, hf),
                                  ap3(mem_q, hf), ap3(acc_q, hf))
            vop(lambda E, yv=yv, sv=sv: E.tensor_single_scalar(
                sv, yv, 1.0, ALU.is_gt))
            if hf == 0 and (p - 1) in E_accT:
                wait("gp", "pe", E_accT[p - 1])
            wait("gp", "gp", n["gp"])
            wait("gp", "ve", n["ve"])
            op("gp", lambda E, sv=sv, av=av: E.tensor_copy(av, sv), inc="gp")
            vop(lambda E, yv=yv, sv=sv, mv=mv: E.scalar_tensor_tensor(
                mv, yv, 1.9, sv, ALU.mult, ALU.subtract))
            for st3 in range(3):
                wait("ve", "gp", n["gp"])
                vop(lambda E, sv=sv, mv=mv: E.tensor_single_scalar(
                    sv, mv, 1.0, ALU.is_gt))
                wait("gp", "gp", n["gp"])
                wait("gp", "ve", n["ve"])
                op("gp", lambda E, sv=sv, av=av: E.tensor_add(av, av, sv),
                   inc="gp")
                vop(lambda E, sv=sv, mv=mv: E.scalar_tensor_tensor(
                    mv, mv, 0.9, sv, ALU.mult, ALU.subtract))
                vop(lambda E, yv=yv, mv=mv: E.scalar_tensor_tensor(
                    mv, mv, 1.0, yv, ALU.mult, ALU.add))
                if st3 == 2 and hf == nhalf - 1:
                    E_spk_yfree[p] = n["ve"]
            wait("ve", "gp", n["gp"])
            vop(lambda E, sv=sv, mv=mv: E.tensor_single_scalar(
                sv, mv, 1.0, ALU.is_gt))
            E_spk_vdone[p] = n["ve"]
            wait("gp", "gp", n["gp"])
            wait("gp", "ve", n["ve"])
            op("gp", lambda E, sv=sv, av=av: E.tensor_add(av, av, sv),
               inc="gp")
            E_spk_gd_half.append(n["gp"])
        E_spk_gdone[p] = n["gp"]

        if p == PF - 1:
            emit_accT(p, E_spk_gd_half)

    for i in range(NSY):
        wait("sy", f"qs{i}", n[f"qs{i}"])
    for i in range(NGP):
        wait("sy", f"qg{i}", n[f"qg{i}"])
    for i in range(NSE):
        wait("sy", f"qe{i}", n[f"qe{i}"])

    with nc.Block() as block:
        @block.tensor
        def _(E):
            for fn in plan["pe"]:
                fn(E)

        @block.vector
        def _(E):
            for fn in plan["ve"]:
                fn(E)

        @block.scalar
        def _(E):
            for fn in plan["se"]:
                fn(E)

        @block.gpsimd
        def _(E):
            for fn in plan["gp"]:
                fn(E)

        @block.sync
        def _(E):
            for fn in plan["sy"]:
                fn(E)

    ctx.close()
    return nc


_NC_CACHE = None


def get_nc():
    global _NC_CACHE
    if _NC_CACHE is None:
        _NC_CACHE = build_nc()
    return _NC_CACHE


def kernel(L0_attn_src, L0_attn_Wq, L0_attn_bq, L0_attn_Wk, L0_attn_bk,
           L0_attn_Wv, L0_attn_bv, L0_attn_Wo, L0_attn_bo,
           L0_ffn_W_fc, L0_ffn_W_proj):
    nc = get_nc()
    x = np.ascontiguousarray(np.asarray(L0_attn_src, dtype=np.float32))
    common = {
        "Wq": np.ascontiguousarray(np.asarray(L0_attn_Wq, np.float32)),
        "Wk": np.ascontiguousarray(np.asarray(L0_attn_Wk, np.float32)),
        "Wv": np.ascontiguousarray(np.asarray(L0_attn_Wv, np.float32)),
        "Wo": np.ascontiguousarray(np.asarray(L0_attn_Wo, np.float32)),
        "bq": np.ascontiguousarray(np.asarray(L0_attn_bq, np.float32)),
        "bk": np.ascontiguousarray(np.asarray(L0_attn_bk, np.float32)),
        "bv": np.ascontiguousarray(np.asarray(L0_attn_bv, np.float32)),
        "bo": np.ascontiguousarray(np.asarray(L0_attn_bo, np.float32)),
        "Wfc": np.ascontiguousarray(np.asarray(L0_ffn_W_fc, np.float32)),
        "Wpr": np.ascontiguousarray(np.asarray(L0_ffn_W_proj, np.float32)),
    }
    in_maps = [dict(common, x=np.ascontiguousarray(x[c])) for c in range(8)]
    res = run_bass_kernel_spmd(nc, in_maps, core_ids=list(range(8)))
    return np.stack([res.results[c]["out"] for c in range(8)], axis=0)


# version 23
# speedup vs baseline: 1.0250x; 1.0051x over previous
from contextlib import ExitStack

import numpy as np

import concourse.bass as bass
import concourse.mybir as mybir
from concourse.bass_utils import run_bass_kernel_spmd

F32 = mybir.dt.float32
BF16 = mybir.dt.bfloat16
ALU = mybir.AluOpType
AX = mybir.AxisListType
ACT = mybir.ActivationFunctionType

T = 2048
D = 1024
PA = 4
TPA = 512
PF = 8
F = 4096
DKS = float(128 ** (-0.5))
EPS = 1e-5


def build_nc():
    nc = bass.Bass("TRN2", target_bir_lowering=False)

    x_hbm = nc.dram_tensor("x", [T, D], F32, kind="ExternalInput").ap()
    wq_h = nc.dram_tensor("Wq", [D, D], F32, kind="ExternalInput").ap()
    wk_h = nc.dram_tensor("Wk", [D, D], F32, kind="ExternalInput").ap()
    wv_h = nc.dram_tensor("Wv", [D, D], F32, kind="ExternalInput").ap()
    wo_h = nc.dram_tensor("Wo", [D, D], F32, kind="ExternalInput").ap()
    bq_h = nc.dram_tensor("bq", [D], F32, kind="ExternalInput").ap()
    bk_h = nc.dram_tensor("bk", [D], F32, kind="ExternalInput").ap()
    bv_h = nc.dram_tensor("bv", [D], F32, kind="ExternalInput").ap()
    bo_h = nc.dram_tensor("bo", [D], F32, kind="ExternalInput").ap()
    wfc_h = nc.dram_tensor("Wfc", [F, D], F32, kind="ExternalInput").ap()
    wpr_h = nc.dram_tensor("Wpr", [D, F], F32, kind="ExternalInput").ap()
    out_hbm = nc.dram_tensor("out", [T, D], F32, kind="ExternalOutput").ap()
    w2d_h = nc.dram_tensor("w2scratch", [32 * 128 * 1024], BF16,
                           kind="Internal").ap()
    w1d_h = nc.dram_tensor("w1scratch", [32 * 128 * 1024], BF16,
                           kind="Internal").ap()

    ctx = ExitStack()
    e = ctx.enter_context

    x_sb = e(nc.sbuf_tensor("x_sb", [128, T * D // 128], F32))
    regB = ExitStack()
    wq_sb = regB.enter_context(nc.sbuf_tensor("wq_sb", [128, 8192], BF16))
    wk_sb = regB.enter_context(nc.sbuf_tensor("wk_sb", [128, 8192], BF16))
    wv_sb = regB.enter_context(nc.sbuf_tensor("wv_sb", [128, 8192], BF16))
    wo_sb = regB.enter_context(nc.sbuf_tensor("wo_sb", [128, 8192], BF16))
    regC = ExitStack()
    xn1_bf = regC.enter_context(nc.sbuf_tensor("xn1_bf", [128, 4096], BF16))
    xn1t_sb = regC.enter_context(nc.sbuf_tensor("xn1t_sb", [128, 4096], BF16))
    q_bf = regC.enter_context(nc.sbuf_tensor("q_bf", [128, 4096], BF16))
    k_bf = regC.enter_context(nc.sbuf_tensor("k_bf", [128, 4096], BF16))
    v_bf = regC.enter_context(nc.sbuf_tensor("v_bf", [128, 4096], BF16))
    attn_sb = regC.enter_context(nc.sbuf_tensor("attn_sb", [128, 4096], BF16))
    s_f32 = [e(nc.sbuf_tensor(f"s_f32_{i}", [128, 256], F32, side="right"))
             for i in range(3)]
    p_bf = [e(nc.sbuf_tensor(f"p_bf_{i}", [128, 256], BF16, side="right"))
            for i in range(3)]
    pt_bf = [e(nc.sbuf_tensor(f"pt_bf_{i}", [128, 256], BF16, side="right"))
             for i in range(3)]
    ident = e(nc.sbuf_tensor("ident", [128, 128], BF16, side="right"))
    wtmp2 = [e(nc.sbuf_tensor(f"wtmp2_{i}", [128, 2048], BF16, side="right"))
             for i in range(2)]
    wtmp1 = [e(nc.sbuf_tensor(f"wtmp1_{i}", [128, 1024], BF16, side="right"))
             for i in range(4)]
    w2stage = [e(nc.sbuf_tensor(f"w2stage_{i}", [128, 1024], BF16,
                                side="right")) for i in range(2)]
    bo_bf = e(nc.sbuf_tensor("bo_bf", [1, 1024], BF16, side="right"))
    ones_bf = e(nc.sbuf_tensor("ones_bf", [1, 128], BF16, side="right"))
    bq_sb = e(nc.sbuf_tensor("bq_sb", [128, 8], F32, side="right"))
    bk_sb = e(nc.sbuf_tensor("bk_sb", [128, 8], F32, side="right"))
    bv_sb = e(nc.sbuf_tensor("bv_sb", [128, 8], F32, side="right"))
    bqs_sb = e(nc.sbuf_tensor("bqs_sb", [128, 8], F32, side="right"))
    sums = e(nc.sbuf_tensor("sums", [128, 4], F32, side="right"))
    sqsums = e(nc.sbuf_tensor("sqsums", [128, 4], F32, side="right"))
    mmean = e(nc.sbuf_tensor("mmean", [128, 4], F32, side="right"))
    nmean = e(nc.sbuf_tensor("nmean", [128, 4], F32, side="right"))
    vtmp = e(nc.sbuf_tensor("vtmp", [128, 4], F32, side="right"))
    veps = e(nc.sbuf_tensor("veps", [128, 4], F32, side="right"))
    rstd = e(nc.sbuf_tensor("rstd", [128, 4], F32, side="right"))
    negmax = e(nc.sbuf_tensor("negmax", [128, 4], F32, side="right"))
    ssum = e(nc.sbuf_tensor("ssum", [128, 12], F32, side="right"))
    rsum = e(nc.sbuf_tensor("rsum", [128, 12], F32, side="right"))

    psA = e(nc.psum_tensor("psA", [128, 1024], F32))
    psB = e(nc.psum_tensor("psB", [128, 1024], F32))
    psC = e(nc.psum_tensor("psC", [128, 1024], F32))
    psD = e(nc.psum_tensor("psD", [128, 2048], BF16))

    S = {nm: e(nc.semaphore(nm)) for nm in ("pe", "ve", "se", "gp")}
    NSY, NGP, NSE = 6, 10, 6
    for i in range(NSY):
        S[f"qs{i}"] = e(nc.semaphore(f"qs{i}"))
    for i in range(NGP):
        S[f"qg{i}"] = e(nc.semaphore(f"qg{i}"))
    for i in range(NSE):
        S[f"qe{i}"] = e(nc.semaphore(f"qe{i}"))

    plan = {k: [] for k in ("pe", "ve", "se", "gp", "sy")}
    n = dict(pe=0, ve=0, se=0, gp=0)
    for i in range(NSY):
        n[f"qs{i}"] = 0
    for i in range(NGP):
        n[f"qg{i}"] = 0
    for i in range(NSE):
        n[f"qe{i}"] = 0

    def emit(eng, fn):
        plan[eng].append(fn)

    def wait(eng, sem, val):
        if val <= 0:
            return
        assert val <= n[sem], (eng, sem, val, n[sem])
        emit(eng, lambda E, sem=sem, val=val: E.wait_ge(S[sem], val))

    def op(eng, build, inc=None):
        if inc is None:
            emit(eng, lambda E, b=build: b(E))
            return None
        n[inc] += 1
        emit(eng, lambda E, b=build, s=inc: b(E).then_inc(S[s], 1))
        return n[inc]

    qidx = {"sy": 0, "gp": 0, "se": 0}

    def dma(eng, build):
        npool = {"sy": NSY, "gp": NGP, "se": NSE}[eng]
        pfx = {"sy": "qs", "gp": "qg", "se": "qe"}[eng]
        k = qidx[eng] % npool
        qidx[eng] += 1
        nm = f"{pfx}{k}"
        if n[nm] > 0:
            wait(eng, nm, n[nm])
        n[nm] += 16
        emit(eng, lambda E, b=build, nm=nm: b(E).then_inc(S[nm], 16))
        return (nm, n[nm])

    op("gp", lambda E: E.iota(ident[:], [[1, 128]], base=0,
                              channel_multiplier=-1,
                              allow_small_or_imprecise_dtypes=True), inc="gp")
    wait("gp", "gp", n["gp"])
    op("gp", lambda E: E.tensor_single_scalar(ident[:], ident[:], 0.0,
                                              ALU.is_equal), inc="gp")
    E_ident = n["gp"]
    op("gp", lambda E: E.memset(ones_bf[:], 1.0), inc="gp")
    E_ones = n["gp"]

    E_xp = []
    E_xp.append(dma("sy", lambda E: E.dma_start(
        out=x_sb[:, 0:4096].rearrange("p (j d) -> p j d", d=D),
        in_=x_hbm[0:TPA, :].rearrange("(j p) d -> p j d", p=128))))
    E_bias = {}
    for nm2, bh, bsb in (("q", bq_h, bq_sb), ("k", bk_h, bk_sb),
                         ("v", bv_h, bv_sb)):
        E_bias[nm2] = dma("sy", lambda E, bh=bh, bsb=bsb: E.dma_start(
            out=bsb[:], in_=bass.AP(bh.tensor, 0, [[1, 128], [128, 8]]),
            allow_slow_non_contiguous=True))
    for p in range(1, PA):
        E_xp.append(dma("sy", lambda E, p=p: E.dma_start(
            out=x_sb[:, p * 4096:(p + 1) * 4096].rearrange(
                "p (j d) -> p j d", d=D),
            in_=x_hbm[p * TPA:(p + 1) * TPA, :].rearrange(
                "(j p) d -> p j d", p=128))))

    E_w = {}
    E_wq_half = dma("gp", lambda E: E.dma_start(
        out=bass.AP(wq_sb[:].tensor, 0, [[8192, 128], [1024, 8], [1, 512]]),
        in_=wq_h[:, 0:512].rearrange("(c p) d -> p c d", p=128)))
    E_w["q"] = dma("gp", lambda E: E.dma_start(
        out=bass.AP(wq_sb[:].tensor, 512, [[8192, 128], [1024, 8], [1, 512]]),
        in_=wq_h[:, 512:1024].rearrange("(c p) d -> p c d", p=128)))
    for nm, wh, wsb in (("k", wk_h, wk_sb), ("v", wv_h, wv_sb),
                        ("o", wo_h, wo_sb)):
        E_w[nm] = dma("gp", lambda E, wh=wh, wsb=wsb: E.dma_start(
            out=wsb[:].rearrange("p (c d) -> p c d", d=D),
            in_=wh.rearrange("(c p) d -> p c d", p=128)))
    E_bo = dma("gp", lambda E: E.dma_start(
        out=bo_bf[:], in_=bass.AP(bo_h.tensor, 0, [[0, 1], [1, 1024]])))

    chunks2 = [(m, fh) for m in range(8) for fh in range(2)]
    E_ld2 = {}
    E_tr2 = {}
    E_st2 = {}
    w2g = {"gidx": 0}

    def w2_dma(ci):
        m, fh = chunks2[ci]
        if ci >= 2:
            wait("gp", "pe", E_tr2[ci - 2])
        E_ld2[ci] = dma("gp", lambda E, m=m, fh=fh, ci=ci: E.dma_start(
            out=wtmp2[ci % 2][:],
            in_=wpr_h[m * 128:(m + 1) * 128, fh * 2048:(fh + 1) * 2048]))


    w2_dma(0)
    w2_dma(1)

    wait("ve", *E_bias["q"])
    wait("ve", *E_bias["k"])
    wait("ve", *E_bias["v"])
    op("ve", lambda E: E.tensor_single_scalar(bqs_sb[:], bq_sb[:], DKS,
                                              ALU.mult), inc="ve")
    E_bqs = n["ve"]

    E_w1ld = {}
    E_w1tr = {}

    def w1_dma(fc):
        if fc >= 4:
            wait("gp", "pe", E_w1tr[fc - 4])
        E_w1ld[fc] = dma("gp", lambda E, fc=fc: E.dma_start(
            out=wtmp1[fc % 4][:], in_=wfc_h[fc * 128:(fc + 1) * 128, :]))

    for fc in range(4):
        w1_dma(fc)

    relQ = [("ve", 0)] * 4
    relD = [("se", 0), ("se", 0)]
    relC2 = [("ve", 0), ("ve", 0)]

    qkv_quarters = [psA[:, 0:512], psA[:, 512:1024],
                    psB[:, 0:512], psB[:, 512:1024]]

    def pe_wait_rel(rel):
        wait("pe", rel[0], rel[1])

    E_x2 = {}
    war = {"tta": 0, "p": 0}

    E_xn1_d = {}
    E_xn1t_fill_prev = [0]

    def ln1_part1(p, jj):
        xtj = x_sb[:, (4 * p + jj) * 1024:(4 * p + jj + 1) * 1024]
        xoj = xn1_bf[:, jj * 1024:(jj + 1) * 1024]
        if jj == 0:
            wait("ve", *E_xp[p])
            wait("se", *E_xp[p])
            wait("se", "pe", E_xn1t_fill_prev[0])
        op("ve", lambda E, xtj=xtj: E.reduce_sum(
            sums[:, jj:jj + 1], xtj, AX.X), inc="ve")
        wait("ve", "ve", n["ve"])
        op("se", lambda E, xtj=xtj, xoj=xoj: E.activation(
            xoj, xtj, ACT.Square, accum_out=sqsums[:, jj:jj + 1]),
            inc="se")

    def ln1_part2(p):
        wait("ve", "se", n["se"])
        for b_ in [
            lambda E: E.tensor_single_scalar(
                nmean[:, :4], sums[:, :4], -1.0 / D, ALU.mult),
            lambda E: E.tensor_single_scalar(
                mmean[:, :4], sums[:, :4], 1.0 / D, ALU.mult),
            lambda E: E.tensor_mul(mmean[:, :4], mmean[:, :4], mmean[:, :4]),
            lambda E: E.scalar_tensor_tensor(
                veps[:, :4], sqsums[:, :4], 1.0 / D, mmean[:, :4],
                ALU.mult, ALU.subtract),
            lambda E: E.tensor_single_scalar(
                veps[:, :4], veps[:, :4], EPS, ALU.add),
        ]:
            op("ve", b_, inc="ve")
            wait("ve", "ve", n["ve"])
        wait("se", "ve", n["ve"])
        op("se", lambda E: E.activation(vtmp[:, :4], veps[:, :4], ACT.Ln),
           inc="se")
        wait("se", "se", n["se"])
        op("se", lambda E: E.activation(rstd[:, :4], vtmp[:, :4], ACT.Exp,
                                        scale=-0.5), inc="se")

    def ln1_part3(p):
        wait("ve", "se", n["se"])
        wait("ve", "pe", E_xn1t_fill_prev[0])
        for jj in range(4):
            xtj = x_sb[:, (4 * p + jj) * 1024:(4 * p + jj + 1) * 1024]
            xoj = xn1_bf[:, jj * 1024:(jj + 1) * 1024]
            op("ve", lambda E, jj=jj, xtj=xtj, xoj=xoj: E.tensor_scalar(
                xoj, xtj, nmean[:, jj:jj + 1], rstd[:, jj:jj + 1],
                ALU.add, ALU.mult), inc="ve" if jj == 3 else None)
        E_xn1_d[p] = n["ve"]

    for jj in range(4):
        ln1_part1(0, jj)
    ln1_part2(0)
    ln1_part3(0)

    E_xn1t_d = {}

    def xn1t_group(p, jj):
        if jj == 0:
            wait("pe", "ve", E_xn1_d[p])
            if p == 0:
                wait("pe", "gp", E_ident)
        pe_wait_rel(relD[jj % 2])
        for c in range(8):
            op("pe", lambda E, jj=jj, c=c: E.transpose(
                psD[:, (jj % 2) * 1024 + c * 128:
                    (jj % 2) * 1024 + (c + 1) * 128],
                xn1_bf[:, jj * 1024 + c * 128:jj * 1024 + (c + 1) * 128],
                ident[:]), inc="pe" if c == 7 else None)
        fill = n["pe"]
        wait("se", "pe", fill)
        dst = bass.AP(xn1t_sb[:].tensor, jj * 128,
                      [[4096, 128], [512, 8], [1, 128]])
        src = psD[:, (jj % 2) * 1024:(jj % 2) * 1024 + 1024].rearrange(
            "p (c t) -> p c t", c=8)
        op("se", lambda E, dst=dst, src=src: E.copy(dst, src),
           inc="se")
        relD[jj % 2] = ("se", n["se"])
        E_xn1t_d[(p, jj)] = n["se"]
        if jj == 3:
            E_xn1t_fill_prev[0] = fill

    for p in range(PA):
        if p == 0:
            for jj in range(4):
                xn1t_group(0, jj)

        E_qe, E_ke = [0] * 8, [0] * 8
        wait("pe", "se", E_xn1t_d[(p, 3)])
        if p == 0:
            wait("pe", *E_wq_half)
        g = 0
        for oc in range(8):
            if p == 0 and oc == 4:
                wait("pe", *E_w["q"])
            qb = qkv_quarters[g % 4]
            pe_wait_rel(relQ[g % 4])
            for c in range(8):
                op("pe", lambda E, oc=oc, c=c, qb=qb: E.matmul(
                    qb, wq_sb[:, c * 1024 + oc * 128:c * 1024 + (oc + 1) * 128],
                    xn1t_sb[:, c * 512:(c + 1) * 512],
                    start=(c == 0), stop=(c == 7)),
                    inc="pe" if c == 7 else None)
            wait("se", "pe", n["pe"])
            if p == 0 and oc == 0:
                wait("se", "ve", E_bqs)
            op("se", lambda E, oc=oc, qb=qb: E.activation(
                q_bf[:, oc * 512:(oc + 1) * 512], qb, ACT.Identity,
                bias=bqs_sb[:, oc:oc + 1], scale=DKS), inc="se")
            relQ[g % 4] = ("se", n["se"])
            E_qe[oc] = n["se"]
            g += 1
        if p == 0:
            wait("pe", *E_w["k"])
        for oc in range(8):
            qb = qkv_quarters[g % 4]
            pe_wait_rel(relQ[g % 4])
            for c in range(8):
                op("pe", lambda E, oc=oc, c=c, qb=qb: E.matmul(
                    qb, wk_sb[:, c * 1024 + oc * 128:c * 1024 + (oc + 1) * 128],
                    xn1t_sb[:, c * 512:(c + 1) * 512],
                    start=(c == 0), stop=(c == 7)),
                    inc="pe" if c == 7 else None)
            wait("se", "pe", n["pe"])
            op("se", lambda E, oc=oc, qb=qb: E.activation(
                k_bf[:, oc * 512:(oc + 1) * 512], qb, ACT.Identity,
                bias=bk_sb[:, oc:oc + 1]), inc="se")
            relQ[g % 4] = ("se", n["se"])
            E_ke[oc] = n["se"]
            g += 1
        if p == 0:
            wait("pe", *E_w["v"])
        for jj in range(4):
            for half in range(2):
                qb = qkv_quarters[g % 4]
                pe_wait_rel(relQ[g % 4])
                for c in range(8):
                    op("pe", lambda E, jj=jj, half=half, c=c, qb=qb: E.matmul(
                        qb,
                        xn1t_sb[:, c * 512 + jj * 128:c * 512 + (jj + 1) * 128],
                        wv_sb[:, c * 1024 + half * 512:
                              c * 1024 + (half + 1) * 512],
                        start=(c == 0), stop=(c == 7)),
                        inc="pe" if c == 7 else None)
                wait("ve", "pe", n["pe"])
                op("ve", lambda E, jj=jj, half=half, qb=qb: E.tensor_copy(
                    v_bf[:, jj * 1024 + half * 512:
                         jj * 1024 + (half + 1) * 512], qb), inc="ve")
                relQ[g % 4] = ("ve", n["ve"])
                g += 1
        E_v_all = n["ve"]

        E_exp4 = [0] * 8
        E_p = [0] * 8
        E_ptev = [0] * 8
        E_pt = [0] * 8
        E_pv = [0] * 8
        E_attnev = [0] * 8

        def scores(h):
            wait("pe", "se", max(E_qe[h], E_ke[h]))
            if h >= 2:
                wait("pe", "se", E_exp4[h - 2])
            else:
                pe_wait_rel(relC2[h % 2])
            for bl in range(8):
                op("pe", lambda E, h=h, bl=bl: E.matmul(
                    psC[(bl // 4) * 64:(bl // 4) * 64 + 64,
                        (h % 2) * 512 + (bl % 4) * 64:
                        (h % 2) * 512 + (bl % 4) * 64 + 64],
                    q_bf[:, h * 512 + bl * 64:h * 512 + (bl + 1) * 64],
                    k_bf[:, h * 512 + bl * 64:h * 512 + (bl + 1) * 64],
                    start=True, stop=True), inc="pe" if bl == 7 else None)
            return n["pe"]

        def exp4_h(h, E_s):
            wait("se", "pe", E_s)
            op("se", lambda E, h=h: E.activation(
                s_f32[h % 3][:],
                psC[:, (h % 2) * 512:(h % 2) * 512 + 256],
                ACT.Exp), inc="se")
            E_exp4[h] = n["se"]

        def softmax_post(h):
            c0 = (h % 3) * 4
            wait("ve", "se", E_exp4[h])
            if h >= 3:
                wait("ve", "pe", E_pt[h - 3])
            g3 = s_f32[h % 3][:].rearrange("p (g k) -> p g k", g=4)
            op("ve", lambda E, g3=g3, c0=c0: E.tensor_reduce(
                ssum[:, c0:c0 + 4], g3, AX.X, ALU.add), inc="ve")
            wait("ve", "ve", n["ve"])
            op("ve", lambda E, c0=c0: E.reciprocal(
                rsum[:, c0:c0 + 4], ssum[:, c0:c0 + 4]), inc="ve")
            wait("ve", "ve", n["ve"])
            op("ve", lambda E, h=h, g3=g3, c0=c0: E.tensor_tensor(
                p_bf[h % 3][:].rearrange("p (g k) -> p g k", g=4), g3,
                rsum[:, c0:c0 + 4].unsqueeze(2).broadcast_to([128, 4, 64]),
                ALU.mult), inc="ve")
            E_p[h] = n["ve"]

        def pt_h(h):
            wait("pe", "ve", E_p[h])
            pe_wait_rel(relD[h % 2])
            base = (h % 2) * 1024
            for t_ in range(2):
                op("pe", lambda E, h=h, t_=t_, base=base: E.transpose(
                    psD[:, base + t_ * 128:base + (t_ + 1) * 128],
                    p_bf[h % 3][:, t_ * 128:(t_ + 1) * 128], ident[:]),
                    inc="pe" if t_ == 1 else None)
            E_pt[h] = n["pe"]

        def ptev_h(h):
            wait("se", "pe", E_pt[h])
            base = (h % 2) * 1024
            op("se", lambda E, h=h, base=base: E.copy(
                pt_bf[h % 3][:], psD[:, base:base + 256]), inc="se")
            E_ptev[h] = n["se"]
            relD[h % 2] = ("se", n["se"])

        def pv_h(h):
            wait("pe", "se", E_ptev[h])
            if h < 2:
                wait("pe", "ve", E_v_all)
                wait("pe", "se", max(E_qe[7], E_ke[7]))
            else:
                wait("pe", "ve", E_attnev[h - 2])
            pair = psA if h % 2 == 0 else psB
            for bl in range(8):
                panel = bl % 4
                tt = panel // 2
                kb = (panel % 2) * 64
                qc = (bl // 4) * 64
                col = (bl % 2) * 512 + (bl // 2) * 64
                op("pe", lambda E, h=h, bl=bl, tt=tt, kb=kb, qc=qc, col=col,
                   pair=pair: E.matmul(
                       pair[:, col:col + 64],
                       v_bf[(bl % 2) * 64:(bl % 2) * 64 + 64,
                            (bl // 2) * 1024 + h * 128:
                            (bl // 2) * 1024 + (h + 1) * 128],
                       pt_bf[h % 3][kb:kb + 64,
                                    tt * 128 + qc:tt * 128 + qc + 64],
                       start=True, stop=True),
                    inc="pe" if bl == 7 else None)
            E_pv[h] = n["pe"]

        def attnev_h(h):
            wait("ve", "pe", E_pv[h])
            pair = psA if h % 2 == 0 else psB
            src = bass.AP(pair[:].tensor, 0,
                          [[1024, 128], [512, 2], [64, 4], [1, 64]])
            dst = bass.AP(attn_sb[:].tensor, h * 512,
                          [[4096, 128], [64, 2], [128, 4], [1, 64]])
            op("ve", lambda E, h=h, src=src, dst=dst: E.tensor_scalar(
                dst, src, bv_sb[:, h:h + 1], None, ALU.add), inc="ve")
            E_attnev[h] = n["ve"]

        def w1_group(fc):
            half = w2g["gidx"] % 2
            pe_wait_rel(relD[half])
            for c in range(8):
                op("pe", lambda E, fc=fc, c=c, half=half: E.transpose(
                    psD[:, half * 1024 + c * 128:half * 1024 + (c + 1) * 128],
                    wtmp1[fc % 4][:, c * 128:(c + 1) * 128], ident[:]),
                    inc="pe" if c == 7 else None)
            fill = n["pe"]
            E_w1tr[fc] = n["pe"]
            if fc + 4 < 32:
                w1_dma(fc + 4)
            wait("se", "pe", fill)
            if w2g["gidx"] >= 2:
                wait("se", *E_st2[w2g["gidx"] - 2])
            op("se", lambda E, half=half, s=w2g["gidx"] % 2: E.copy(
                w2stage[s][:], psD[:, half * 1024:half * 1024 + 1024]),
                inc="se")
            relD[half] = ("se", n["se"])
            wait("se", "se", n["se"])
            dst = bass.AP(w1d_h.tensor, fc * 131072, [[1024, 128], [1, 1024]])
            E_st2[w2g["gidx"]] = dma("se", lambda E, dst=dst,
                                     s=w2g["gidx"] % 2:
                                     E.dma_start(out=dst, in_=w2stage[s][:]))
            w2g["gidx"] += 1

        def w2_group(ci, grp):
            half = w2g["gidx"] % 2
            pe_wait_rel(relD[half])
            for t_ in range(8):
                tl = grp * 8 + t_
                op("pe", lambda E, ci=ci, tl=tl, half=half: E.transpose(
                    psD[:, half * 1024 + (tl % 8) * 128:
                        half * 1024 + (tl % 8 + 1) * 128],
                    wtmp2[ci % 2][:, tl * 128:(tl + 1) * 128], ident[:]),
                    inc="pe" if t_ == 7 else None)
            fill = n["pe"]
            m, fh = chunks2[ci]
            wait("ve", "pe", fill)
            if w2g["gidx"] >= 2:
                wait("ve", *E_st2[w2g["gidx"] - 2])
            op("ve", lambda E, half=half, s=w2g["gidx"] % 2: E.tensor_copy(
                w2stage[s][:], psD[:, half * 1024:half * 1024 + 1024]),
                inc="ve")
            relD[half] = ("ve", n["ve"])
            base = (fh * 16 + grp * 8) * 131072 + m * 128
            dst = bass.AP(w2d_h.tensor, base,
                          [[1024, 128], [131072, 8], [1, 128]])
            wait("sy", "ve", n["ve"])
            E_st2[w2g["gidx"]] = dma("sy", lambda E, dst=dst,
                                     s=w2g["gidx"] % 2:
                                     E.dma_start(out=dst, in_=w2stage[s][:]))
            w2g["gidx"] += 1
            if grp == 1:
                E_tr2[ci] = n["pe"]
                if ci + 2 < len(chunks2):
                    w2_dma(ci + 2)

        E_s = scores(0)
        exp4_h(0, E_s)
        E_s = scores(1)
        exp4_h(1, E_s)
        softmax_post(0)
        pt_h(0)
        ptev_h(0)
        E_s = scores(2)
        exp4_h(2, E_s)
        softmax_post(1)
        pt_h(1)
        ptev_h(1)
        for h in range(8):
            if h + 2 < 8:
                softmax_post(h + 2)
                pt_h(h + 2)
                ptev_h(h + 2)
            if h + 3 < 8:
                E_s = scores(h + 3)
                exp4_h(h + 3, E_s)
            pv_h(h)
            attnev_h(h)
            ci = 4 * p + h // 2
            if h % 2 == 0:
                wait("pe", *E_ld2[ci])
            w2_group(ci, h % 2)
            fcw = 8 * p + h
            wait("pe", *E_w1ld[fcw])
            w1_group(fcw)
            if p + 1 < PA:
                if 1 <= h <= 4:
                    ln1_part1(p + 1, h - 1)
                elif h == 5:
                    ln1_part2(p + 1)
                elif h == 6:
                    ln1_part3(p + 1)
        relC2[0] = ("se", E_exp4[6])
        relC2[1] = ("se", E_exp4[7])

        if p == 0:
            wait("pe", *E_bo)
            wait("pe", *E_w["o"])
            wait("pe", "gp", E_ones)
        wait("pe", "ve", E_attnev[7])
        for jj in range(4):
            pair = psA if jj % 2 == 0 else psB
            if jj >= 2:
                wait("pe", "ve", E_x2[4 * p + jj - 2])
            for half in range(2):
                for c in range(8):
                    op("pe", lambda E, jj=jj, half=half, c=c, pair=pair:
                       E.matmul(
                           pair[:, half * 512:(half + 1) * 512],
                           attn_sb[:, c * 512 + jj * 128:
                                   c * 512 + (jj + 1) * 128],
                           wo_sb[:, c * 1024 + half * 512:
                                 c * 1024 + (half + 1) * 512],
                           start=(c == 0), stop=False))
                op("pe", lambda E, half=half, pair=pair: E.matmul(
                    pair[:, half * 512:(half + 1) * 512], ones_bf[:],
                    bo_bf[0:1, half * 512:(half + 1) * 512],
                    start=False, stop=True), inc="pe" if half == 1 else None)
            E_opj = n["pe"]
            if p + 1 < PA:
                xn1t_group(p + 1, jj)
            wait("ve", "pe", E_opj)
            jg = 4 * p + jj
            for half in range(2):
                xs = x_sb[:, jg * 1024 + half * 512:jg * 1024 + (half + 1) * 512]
                op("ve", lambda E, xs=xs, pair=pair, half=half:
                   E.scalar_tensor_tensor(
                       xs, pair[:, half * 512:(half + 1) * 512], 1.0, xs,
                       ALU.mult, ALU.add), inc="ve" if half == 1 else None)
            E_x2[jg] = n["ve"]
        relQ[0] = relQ[1] = ("ve", E_x2[4 * p + 2])
        relQ[2] = relQ[3] = ("ve", E_x2[4 * p + 3])

    E_x2_all = n["ve"]
    E_attn_end_pe = n["pe"]

    regC.close()
    regB.close()
    E_w2_all = [(f"qe{i}", n[f"qe{i}"]) for i in range(NSE)]

    xn2_bf = [e(nc.sbuf_tensor(f"xn2_bf_{i}", [128, 1024], BF16))
              for i in range(2)]
    xn2t_sb = e(nc.sbuf_tensor("xn2t_sb", [128, 2048], BF16))
    h_bf = [e(nc.sbuf_tensor(f"h_bf_{i}", [128, 256], BF16)) for i in range(2)]
    w2slab = [e(nc.sbuf_tensor(f"w2slab_{i}", [128, 2048], BF16))
              for i in range(4)]
    w1slab = [e(nc.sbuf_tensor(f"w1slab_{i}", [128, 2048], BF16))
              for i in range(4)]
    y_sb = e(nc.sbuf_tensor("y_sb", [128, 2048], F32))
    mem_q = e(nc.sbuf_tensor("mem_q", [128, 2048], F32))
    spk_q = e(nc.sbuf_tensor("spk_q", [128, 2048], BF16))
    acc_q = e(nc.sbuf_tensor("acc_q", [128, 2048], BF16))

    relH = relC2
    relD6 = [relD[0]]
    relD7 = [relD[1]]
    E_slab1 = {}

    def slab1_load(p, pr, prev_fc):
        if p == 0 and pr == 0:
            for sm, vv in E_w2_all:
                wait("sy", sm, vv)
        if prev_fc:
            wait("sy", "pe", prev_fc)
        ev = dma("sy", lambda E, pr=pr: E.dma_start(
            out=w1slab[pr % 4][:].rearrange("p (f c) -> p f c", f=2),
            in_=bass.AP(w1d_h.tensor, pr * 262144,
                        [[1024, 128], [131072, 2], [1, 1024]])))
        E_slab1[(p, 2 * pr)] = ev
        E_slab1[(p, 2 * pr + 1)] = ev

    E_yev = {}
    E_spk_yfree = {}
    E_spk_vdone = {}
    E_spk_gdone = {}
    E_final = {}
    E_accT = {}
    E_w1ld = {}
    E_w1ev = {}
    E_xn2t = {}
    E_norm2_d = {}
    E_proj_pe = {}
    E_slab = {}
    E_xn2t_fill = [0, 0]

    def ln2_tile(p, jj):
        jg = 2 * p + jj
        xs = x_sb[:, jg * 1024:(jg + 1) * 1024]
        if p == 0 and jj == 0:
            wait("ve", "ve", E_x2_all)
            wait("se", "ve", E_x2_all)
        op("ve", lambda E, xs=xs: E.reduce_sum(sums[:, 0:1], xs, AX.X),
           inc="ve")
        E_red = n["ve"]
        wait("se", "pe", E_xn2t_fill[jj])
        op("se", lambda E, xs=xs, jj=jj: E.activation(
            xn2_bf[jj][:], xs, ACT.Square, accum_out=sqsums[:, 0:1]),
            inc="se")
        E_sq2 = n["se"]
        wait("ve", "ve", E_red)
        wait("ve", "se", E_sq2)
        for b_ in [
            lambda E: E.tensor_single_scalar(
                nmean[:, 0:1], sums[:, 0:1], -1.0 / D, ALU.mult),
            lambda E: E.tensor_single_scalar(
                mmean[:, 0:1], sums[:, 0:1], 1.0 / D, ALU.mult),
            lambda E: E.tensor_mul(mmean[:, 0:1], mmean[:, 0:1],
                                   mmean[:, 0:1]),
            lambda E: E.scalar_tensor_tensor(
                veps[:, 0:1], sqsums[:, 0:1], 1.0 / D, mmean[:, 0:1],
                ALU.mult, ALU.subtract),
            lambda E: E.tensor_single_scalar(
                veps[:, 0:1], veps[:, 0:1], EPS, ALU.add),
        ]:
            op("ve", b_, inc="ve")
            wait("ve", "ve", n["ve"])
        E_veps2 = n["ve"]
        wait("se", "ve", E_veps2)
        op("se", lambda E: E.activation(vtmp[:, 0:1], veps[:, 0:1],
                                        ACT.Ln), inc="se")
        wait("se", "se", n["se"])
        op("se", lambda E: E.activation(rstd[:, 0:1], vtmp[:, 0:1],
                                        ACT.Exp, scale=-0.5), inc="se")
        E_rstd2 = n["se"]
        wait("ve", "se", E_rstd2)
        op("ve", lambda E, xs=xs, jj=jj: E.tensor_scalar(
            xn2_bf[jj][:], xs, nmean[:, 0:1], rstd[:, 0:1],
            ALU.add, ALU.mult), inc="ve")
        E_norm2_d[(p, jj)] = n["ve"]

    def ln2_transposes(p, jj):
        wait("pe", "ve", E_norm2_d[(p, jj)])
        pe_wait_rel(relD6[0])
        for c in range(8):
            op("pe", lambda E, jj=jj, c=c: E.transpose(
                psD[:, c * 128:(c + 1) * 128],
                xn2_bf[jj][:, c * 128:(c + 1) * 128], ident[:]),
                inc="pe" if c == 7 else None)
        E_xn2t_fill[jj] = n["pe"]
        wait("se", "pe", E_xn2t_fill[jj])
        dst = bass.AP(xn2t_sb[:].tensor, jj * 128,
                      [[2048, 128], [256, 8], [1, 128]])
        src2 = psD[:, 0:1024].rearrange("p (c t) -> p c t", c=8)
        op("se", lambda E, dst=dst, src2=src2: E.copy(dst, src2),
           inc="se")
        relD6[0] = ("se", n["se"])
        E_xn2t[p] = n["se"]

    def slab_load(p, pr, prev_proj):
        if p == 0 and pr == 0:
            for sm, vv in E_w2_all:
                wait("sy", sm, vv)
        if prev_proj:
            wait("sy", "pe", prev_proj)
        ev = dma("sy", lambda E, pr=pr: E.dma_start(
            out=w2slab[pr % 4][:].rearrange("p (f c) -> p f c", f=2),
            in_=bass.AP(w2d_h.tensor, pr * 262144,
                        [[1024, 128], [131072, 2], [1, 1024]])))
        E_slab[(p, 2 * pr)] = ev
        E_slab[(p, 2 * pr + 1)] = ev

    ln2_tile(0, 0)
    ln2_tile(0, 1)
    for pr in range(4):
        slab_load(0, pr, 0)
        slab1_load(0, pr, 0)

    def emit_accT(pp, gd=None):
        for jj in range(2):
            wait("pe", "gp", E_spk_gdone[pp] if gd is None else gd[jj])
            prev = E_final.get((pp, jj - 1), E_final.get((pp - 1, 1), 0))
            wait("pe", "ve", prev)
            for c in range(8):
                op("pe", lambda E, jj=jj, c=c: E.transpose(
                    psD[:, 1024 + c * 128:1024 + (c + 1) * 128],
                    acc_q[:, c * 256 + jj * 128:c * 256 + (jj + 1) * 128],
                    ident[:]), inc="pe" if c == 7 else None)
            fill = n["pe"]
            jg = 2 * pp + jj
            xs = x_sb[:, jg * 1024:(jg + 1) * 1024]
            wait("ve", "pe", fill)
            op("ve", lambda E, xs=xs: E.scalar_tensor_tensor(
                xs, psD[:, 1024:2048], 0.2, xs, ALU.mult, ALU.add), inc="ve")
            E_final[(pp, jj)] = n["ve"]
            wait("gp", "ve", n["ve"])
            dma("gp", lambda E, xs=xs, jg=jg: E.dma_start(
                out=out_hbm[jg * 128:(jg + 1) * 128, :], in_=xs))
        E_accT[pp] = n["pe"]

    for p in range(PF):
        E_sqf = {}
        ln2_transposes(p, 0)
        ln2_transposes(p, 1)
        wait("pe", "se", E_xn2t[p])
        for step in range(33):
            fc = step
            if fc < 32:
                eng, cnt = relH[fc % 2]
                if cnt > 0:
                    wait("pe", eng, cnt)
                wait("pe", *E_slab1[(p, fc)])
                hp = psC[:, (fc % 2) * 512:(fc % 2) * 512 + 256]
                s1b = w1slab[(fc // 2) % 4][:, (fc % 2) * 1024:
                                            (fc % 2) * 1024 + 1024]
                for c in range(8):
                    op("pe", lambda E, fc=fc, c=c, hp=hp, s1b=s1b: E.matmul(
                        hp, s1b[:, c * 128:(c + 1) * 128],
                        xn2t_sb[:, c * 256:(c + 1) * 256],
                        start=(c == 0), stop=(c == 7)),
                        inc="pe" if c == 7 else None)
                E_fcf = n["pe"]
                if fc % 2 == 1 and fc // 2 + 4 < 16:
                    slab1_load(p, fc // 2 + 4, E_fcf)
                if p + 1 < PF and fc in (24, 28):
                    ln2_tile(p + 1, (fc - 24) // 4)
                wait("se", "pe", E_fcf)
                op("se", lambda E, fc=fc, hp=hp: E.activation(
                    h_bf[fc % 2][:], hp, ACT.Relu), inc="se")
                wait("se", "se", n["se"])
                op("se", lambda E, fc=fc: E.activation(
                    h_bf[fc % 2][:], h_bf[fc % 2][:], ACT.Square), inc="se")
                E_sqf[fc] = n["se"]
                relH[fc % 2] = ("se", E_sqf[fc])
            if step >= 1:
                fp = step - 1
                wait("pe", "se", E_sqf[fp])
                wait("pe", *E_slab[(p, fp)])
                if fp == 0:
                    if p > 0:
                        for q in range(4):
                            wait("pe", "se", E_yev[(p - 1, q)])
                    else:
                        wait("pe", "ve", E_x2_all)
                slb = w2slab[(fp // 2) % 4][:, (fp % 2) * 1024:
                                            (fp % 2) * 1024 + 1024]
                for m in range(8):
                    tgt = psA if m < 4 else psB
                    mc = m % 4
                    op("pe", lambda E, fp=fp, m=m, tgt=tgt, mc=mc, slb=slb:
                       E.matmul(
                           tgt[:, mc * 256:(mc + 1) * 256],
                           slb[:, m * 128:(m + 1) * 128],
                           h_bf[fp % 2][:],
                           start=(fp == 0 and m % 2 == 0),
                           stop=(fp == 31 and m % 2 == 1)),
                       inc="pe" if m == 7 else None)
                E_proj_pe[(p, fp)] = n["pe"]
                if fp % 2 == 1 and fp // 2 + 4 < 16:
                    slab_load(p, fp // 2 + 4, E_proj_pe[(p, fp)])
        E_y_fill = n["pe"]
        if p + 1 < PF:
            for j in range(4):
                slab_load(p + 1, j, E_proj_pe[(p, min(25 + 2 * j, 31))])
                slab1_load(p + 1, j, E_y_fill)
        if p > 0:
            emit_accT(p - 1)

        for q in range(4):
            ps = psA if q < 2 else psB
            half = q % 2
            wait("se", "pe", E_y_fill)
            if (p - 1) in E_spk_yfree:
                wait("se", "ve", E_spk_yfree[p - 1])
            op("se", lambda E, ps=ps, half=half, q=q: E.copy(
                y_sb[:, q * 512:(q + 1) * 512],
                ps[:, half * 512:(half + 1) * 512]), inc="se")
            E_yev[(p, q)] = n["se"]

        wait("ve", "se", E_yev[(p, 3)])
        if (p - 1) in E_spk_vdone:
            wait("ve", "ve", E_spk_vdone[p - 1])
        if (p - 1) in E_spk_gdone:
            wait("ve", "gp", E_spk_gdone[p - 1])

        def vop(build):
            op("ve", build, inc="ve")
            wait("ve", "ve", n["ve"])

        nhalf = 2 if p == PF - 1 else 1
        E_spk_gd_half = []

        def ap3(t, hf):
            return bass.AP(t[:].tensor, hf * 128,
                           [[2048, 128], [256, 8], [1, 128]])

        for hf in range(nhalf):
            if nhalf == 1:
                yv, sv, mv, av = y_sb[:], spk_q[:], mem_q[:], acc_q[:]
            else:
                yv, sv, mv, av = (ap3(y_sb, hf), ap3(spk_q, hf),
                                  ap3(mem_q, hf), ap3(acc_q, hf))
            vop(lambda E, yv=yv, sv=sv: E.tensor_single_scalar(
                sv, yv, 1.0, ALU.is_gt))
            if hf == 0 and (p - 1) in E_accT:
                wait("gp", "pe", E_accT[p - 1])
            wait("gp", "gp", n["gp"])
            wait("gp", "ve", n["ve"])
            op("gp", lambda E, sv=sv, av=av: E.tensor_copy(av, sv), inc="gp")
            vop(lambda E, yv=yv, sv=sv, mv=mv: E.scalar_tensor_tensor(
                mv, yv, 1.9, sv, ALU.mult, ALU.subtract))
            for st3 in range(3):
                wait("ve", "gp", n["gp"])
                vop(lambda E, sv=sv, mv=mv: E.tensor_single_scalar(
                    sv, mv, 1.0, ALU.is_gt))
                wait("gp", "gp", n["gp"])
                wait("gp", "ve", n["ve"])
                op("gp", lambda E, sv=sv, av=av: E.tensor_add(av, av, sv),
                   inc="gp")
                vop(lambda E, sv=sv, mv=mv: E.scalar_tensor_tensor(
                    mv, mv, 0.9, sv, ALU.mult, ALU.subtract))
                vop(lambda E, yv=yv, mv=mv: E.scalar_tensor_tensor(
                    mv, mv, 1.0, yv, ALU.mult, ALU.add))
                if st3 == 2 and hf == nhalf - 1:
                    E_spk_yfree[p] = n["ve"]
            wait("ve", "gp", n["gp"])
            vop(lambda E, sv=sv, mv=mv: E.tensor_single_scalar(
                sv, mv, 1.0, ALU.is_gt))
            E_spk_vdone[p] = n["ve"]
            wait("gp", "gp", n["gp"])
            wait("gp", "ve", n["ve"])
            op("gp", lambda E, sv=sv, av=av: E.tensor_add(av, av, sv),
               inc="gp")
            E_spk_gd_half.append(n["gp"])
        E_spk_gdone[p] = n["gp"]

        if p == PF - 1:
            emit_accT(p, E_spk_gd_half)

    for i in range(NSY):
        wait("sy", f"qs{i}", n[f"qs{i}"])
    for i in range(NGP):
        wait("sy", f"qg{i}", n[f"qg{i}"])
    for i in range(NSE):
        wait("sy", f"qe{i}", n[f"qe{i}"])

    with nc.Block() as block:
        @block.tensor
        def _(E):
            for fn in plan["pe"]:
                fn(E)

        @block.vector
        def _(E):
            for fn in plan["ve"]:
                fn(E)

        @block.scalar
        def _(E):
            for fn in plan["se"]:
                fn(E)

        @block.gpsimd
        def _(E):
            for fn in plan["gp"]:
                fn(E)

        @block.sync
        def _(E):
            for fn in plan["sy"]:
                fn(E)

    ctx.close()
    return nc


_NC_CACHE = None


def get_nc():
    global _NC_CACHE
    if _NC_CACHE is None:
        _NC_CACHE = build_nc()
    return _NC_CACHE


def kernel(L0_attn_src, L0_attn_Wq, L0_attn_bq, L0_attn_Wk, L0_attn_bk,
           L0_attn_Wv, L0_attn_bv, L0_attn_Wo, L0_attn_bo,
           L0_ffn_W_fc, L0_ffn_W_proj):
    nc = get_nc()
    x = np.ascontiguousarray(np.asarray(L0_attn_src, dtype=np.float32))
    common = {
        "Wq": np.ascontiguousarray(np.asarray(L0_attn_Wq, np.float32)),
        "Wk": np.ascontiguousarray(np.asarray(L0_attn_Wk, np.float32)),
        "Wv": np.ascontiguousarray(np.asarray(L0_attn_Wv, np.float32)),
        "Wo": np.ascontiguousarray(np.asarray(L0_attn_Wo, np.float32)),
        "bq": np.ascontiguousarray(np.asarray(L0_attn_bq, np.float32)),
        "bk": np.ascontiguousarray(np.asarray(L0_attn_bk, np.float32)),
        "bv": np.ascontiguousarray(np.asarray(L0_attn_bv, np.float32)),
        "bo": np.ascontiguousarray(np.asarray(L0_attn_bo, np.float32)),
        "Wfc": np.ascontiguousarray(np.asarray(L0_ffn_W_fc, np.float32)),
        "Wpr": np.ascontiguousarray(np.asarray(L0_ffn_W_proj, np.float32)),
    }
    in_maps = [dict(common, x=np.ascontiguousarray(x[c])) for c in range(8)]
    res = run_bass_kernel_spmd(nc, in_maps, core_ids=list(range(8)))
    return np.stack([res.results[c]["out"] for c in range(8)], axis=0)


# version 24
# speedup vs baseline: 1.0377x; 1.0124x over previous
from contextlib import ExitStack

import numpy as np

import concourse.bass as bass
import concourse.mybir as mybir
from concourse.bass_utils import run_bass_kernel_spmd

F32 = mybir.dt.float32
BF16 = mybir.dt.bfloat16
ALU = mybir.AluOpType
AX = mybir.AxisListType
ACT = mybir.ActivationFunctionType

T = 2048
D = 1024
PA = 4
TPA = 512
PF = 8
F = 4096
DKS = float(128 ** (-0.5))
EPS = 1e-5


def build_nc():
    nc = bass.Bass("TRN2", target_bir_lowering=False)

    x_hbm = nc.dram_tensor("x", [T, D], F32, kind="ExternalInput").ap()
    wq_h = nc.dram_tensor("Wq", [D, D], F32, kind="ExternalInput").ap()
    wk_h = nc.dram_tensor("Wk", [D, D], F32, kind="ExternalInput").ap()
    wv_h = nc.dram_tensor("Wv", [D, D], F32, kind="ExternalInput").ap()
    wo_h = nc.dram_tensor("Wo", [D, D], F32, kind="ExternalInput").ap()
    bq_h = nc.dram_tensor("bq", [D], F32, kind="ExternalInput").ap()
    bk_h = nc.dram_tensor("bk", [D], F32, kind="ExternalInput").ap()
    bv_h = nc.dram_tensor("bv", [D], F32, kind="ExternalInput").ap()
    bo_h = nc.dram_tensor("bo", [D], F32, kind="ExternalInput").ap()
    wfc_h = nc.dram_tensor("Wfc", [F, D], F32, kind="ExternalInput").ap()
    wpr_h = nc.dram_tensor("Wpr", [D, F], F32, kind="ExternalInput").ap()
    out_hbm = nc.dram_tensor("out", [T, D], F32, kind="ExternalOutput").ap()
    w2d_h = nc.dram_tensor("w2scratch", [32 * 128 * 1024], BF16,
                           kind="Internal").ap()
    w1d_h = nc.dram_tensor("w1scratch", [32 * 128 * 1024], BF16,
                           kind="Internal").ap()

    ctx = ExitStack()
    e = ctx.enter_context

    x_sb = e(nc.sbuf_tensor("x_sb", [128, T * D // 128], F32))
    regB = ExitStack()
    wq_sb = regB.enter_context(nc.sbuf_tensor("wq_sb", [128, 8192], BF16))
    wk_sb = regB.enter_context(nc.sbuf_tensor("wk_sb", [128, 8192], BF16))
    wv_sb = regB.enter_context(nc.sbuf_tensor("wv_sb", [128, 8192], BF16))
    wo_sb = regB.enter_context(nc.sbuf_tensor("wo_sb", [128, 8192], BF16))
    regC = ExitStack()
    xn1_bf = regC.enter_context(nc.sbuf_tensor("xn1_bf", [128, 4096], BF16))
    xn1t_sb = regC.enter_context(nc.sbuf_tensor("xn1t_sb", [128, 4096], BF16))
    q_bf = regC.enter_context(nc.sbuf_tensor("q_bf", [128, 4096], BF16))
    k_bf = regC.enter_context(nc.sbuf_tensor("k_bf", [128, 4096], BF16))
    v_bf = regC.enter_context(nc.sbuf_tensor("v_bf", [128, 4096], BF16))
    attn_sb = regC.enter_context(nc.sbuf_tensor("attn_sb", [128, 4096], BF16))
    s_f32 = [e(nc.sbuf_tensor(f"s_f32_{i}", [128, 256], F32, side="right"))
             for i in range(3)]
    p_bf = [e(nc.sbuf_tensor(f"p_bf_{i}", [128, 256], BF16, side="right"))
            for i in range(3)]
    pt_bf = [e(nc.sbuf_tensor(f"pt_bf_{i}", [128, 256], BF16, side="right"))
             for i in range(3)]
    ident = e(nc.sbuf_tensor("ident", [128, 128], BF16, side="right"))
    wtmp2 = [e(nc.sbuf_tensor(f"wtmp2_{i}", [128, 2048], BF16, side="right"))
             for i in range(2)]
    wtmp1 = [e(nc.sbuf_tensor(f"wtmp1_{i}", [128, 1024], BF16, side="right"))
             for i in range(4)]
    w2stage = [e(nc.sbuf_tensor(f"w2stage_{i}", [128, 1024], BF16,
                                side="right")) for i in range(2)]
    bo_bf = e(nc.sbuf_tensor("bo_bf", [1, 1024], BF16, side="right"))
    ones_bf = e(nc.sbuf_tensor("ones_bf", [1, 128], BF16, side="right"))
    bq_sb = e(nc.sbuf_tensor("bq_sb", [128, 8], F32, side="right"))
    bk_sb = e(nc.sbuf_tensor("bk_sb", [128, 8], F32, side="right"))
    bv_sb = e(nc.sbuf_tensor("bv_sb", [128, 8], F32, side="right"))
    bqs_sb = e(nc.sbuf_tensor("bqs_sb", [128, 8], F32, side="right"))
    sums = e(nc.sbuf_tensor("sums", [128, 4], F32, side="right"))
    sqsums = e(nc.sbuf_tensor("sqsums", [128, 4], F32, side="right"))
    mmean = e(nc.sbuf_tensor("mmean", [128, 4], F32, side="right"))
    nmean = e(nc.sbuf_tensor("nmean", [128, 4], F32, side="right"))
    vtmp = e(nc.sbuf_tensor("vtmp", [128, 4], F32, side="right"))
    veps = e(nc.sbuf_tensor("veps", [128, 4], F32, side="right"))
    rstd = e(nc.sbuf_tensor("rstd", [128, 4], F32, side="right"))
    negmax = e(nc.sbuf_tensor("negmax", [128, 4], F32, side="right"))
    ssum = e(nc.sbuf_tensor("ssum", [128, 12], F32, side="right"))
    rsum = e(nc.sbuf_tensor("rsum", [128, 12], F32, side="right"))

    psA = e(nc.psum_tensor("psA", [128, 1024], F32))
    psB = e(nc.psum_tensor("psB", [128, 1024], F32))
    psC = e(nc.psum_tensor("psC", [128, 1024], F32))
    psD = e(nc.psum_tensor("psD", [128, 2048], BF16))

    S = {nm: e(nc.semaphore(nm)) for nm in ("pe", "ve", "se", "gp")}
    NSY, NGP, NSE = 6, 10, 6
    for i in range(NSY):
        S[f"qs{i}"] = e(nc.semaphore(f"qs{i}"))
    for i in range(NGP):
        S[f"qg{i}"] = e(nc.semaphore(f"qg{i}"))
    for i in range(NSE):
        S[f"qe{i}"] = e(nc.semaphore(f"qe{i}"))

    plan = {k: [] for k in ("pe", "ve", "se", "gp", "sy")}
    n = dict(pe=0, ve=0, se=0, gp=0)
    for i in range(NSY):
        n[f"qs{i}"] = 0
    for i in range(NGP):
        n[f"qg{i}"] = 0
    for i in range(NSE):
        n[f"qe{i}"] = 0

    def emit(eng, fn):
        plan[eng].append(fn)

    def wait(eng, sem, val):
        if val <= 0:
            return
        assert val <= n[sem], (eng, sem, val, n[sem])
        emit(eng, lambda E, sem=sem, val=val: E.wait_ge(S[sem], val))

    def op(eng, build, inc=None):
        if inc is None:
            emit(eng, lambda E, b=build: b(E))
            return None
        n[inc] += 1
        emit(eng, lambda E, b=build, s=inc: b(E).then_inc(S[s], 1))
        return n[inc]

    qidx = {"sy": 0, "gp": 0, "se": 0}

    def dma(eng, build):
        npool = {"sy": NSY, "gp": NGP, "se": NSE}[eng]
        pfx = {"sy": "qs", "gp": "qg", "se": "qe"}[eng]
        k = qidx[eng] % npool
        qidx[eng] += 1
        nm = f"{pfx}{k}"
        if n[nm] > 0:
            wait(eng, nm, n[nm])
        n[nm] += 16
        emit(eng, lambda E, b=build, nm=nm: b(E).then_inc(S[nm], 16))
        return (nm, n[nm])

    op("gp", lambda E: E.iota(ident[:], [[1, 128]], base=0,
                              channel_multiplier=-1,
                              allow_small_or_imprecise_dtypes=True), inc="gp")
    wait("gp", "gp", n["gp"])
    op("gp", lambda E: E.tensor_single_scalar(ident[:], ident[:], 0.0,
                                              ALU.is_equal), inc="gp")
    E_ident = n["gp"]
    op("gp", lambda E: E.memset(ones_bf[:], 1.0), inc="gp")
    E_ones = n["gp"]

    E_xp = []
    E_xp.append(dma("sy", lambda E: E.dma_start(
        out=x_sb[:, 0:4096].rearrange("p (j d) -> p j d", d=D),
        in_=x_hbm[0:TPA, :].rearrange("(j p) d -> p j d", p=128))))
    E_bias = {}
    for nm2, bh, bsb in (("q", bq_h, bq_sb), ("k", bk_h, bk_sb),
                         ("v", bv_h, bv_sb)):
        E_bias[nm2] = dma("sy", lambda E, bh=bh, bsb=bsb: E.dma_start(
            out=bsb[:], in_=bass.AP(bh.tensor, 0, [[1, 128], [128, 8]]),
            allow_slow_non_contiguous=True))
    for p in range(1, PA):
        E_xp.append(dma("sy", lambda E, p=p: E.dma_start(
            out=x_sb[:, p * 4096:(p + 1) * 4096].rearrange(
                "p (j d) -> p j d", d=D),
            in_=x_hbm[p * TPA:(p + 1) * TPA, :].rearrange(
                "(j p) d -> p j d", p=128))))

    E_w = {}
    E_wq_half = dma("gp", lambda E: E.dma_start(
        out=bass.AP(wq_sb[:].tensor, 0, [[8192, 128], [1024, 8], [1, 512]]),
        in_=wq_h[:, 0:512].rearrange("(c p) d -> p c d", p=128)))
    E_w["q"] = dma("gp", lambda E: E.dma_start(
        out=bass.AP(wq_sb[:].tensor, 512, [[8192, 128], [1024, 8], [1, 512]]),
        in_=wq_h[:, 512:1024].rearrange("(c p) d -> p c d", p=128)))
    for nm, wh, wsb in (("k", wk_h, wk_sb), ("v", wv_h, wv_sb),
                        ("o", wo_h, wo_sb)):
        E_w[nm] = dma("gp", lambda E, wh=wh, wsb=wsb: E.dma_start(
            out=wsb[:].rearrange("p (c d) -> p c d", d=D),
            in_=wh.rearrange("(c p) d -> p c d", p=128)))
    E_bo = dma("gp", lambda E: E.dma_start(
        out=bo_bf[:], in_=bass.AP(bo_h.tensor, 0, [[0, 1], [1, 1024]])))

    chunks2 = [(m, fh) for m in range(8) for fh in range(2)]
    E_ld2 = {}
    E_tr2 = {}
    E_st2 = {}
    w2g = {"gidx": 0}

    def w2_dma(ci):
        m, fh = chunks2[ci]
        if ci >= 2:
            wait("gp", "pe", E_tr2[ci - 2])
        E_ld2[ci] = dma("gp", lambda E, m=m, fh=fh, ci=ci: E.dma_start(
            out=wtmp2[ci % 2][:],
            in_=wpr_h[m * 128:(m + 1) * 128, fh * 2048:(fh + 1) * 2048]))


    w2_dma(0)
    w2_dma(1)

    wait("ve", *E_bias["q"])
    wait("ve", *E_bias["k"])
    wait("ve", *E_bias["v"])
    op("ve", lambda E: E.tensor_single_scalar(bqs_sb[:], bq_sb[:], DKS,
                                              ALU.mult), inc="ve")
    E_bqs = n["ve"]

    E_w1ld = {}
    E_w1tr = {}

    def w1_dma(fc):
        if fc >= 4:
            wait("gp", "pe", E_w1tr[fc - 4])
        E_w1ld[fc] = dma("gp", lambda E, fc=fc: E.dma_start(
            out=wtmp1[fc % 4][:], in_=wfc_h[fc * 128:(fc + 1) * 128, :]))

    for fc in range(4):
        w1_dma(fc)

    relQ = [("ve", 0)] * 4
    relD = [("se", 0), ("se", 0)]
    relC2 = [("ve", 0), ("ve", 0)]

    qkv_quarters = [psA[:, 0:512], psA[:, 512:1024],
                    psB[:, 0:512], psB[:, 512:1024]]

    def pe_wait_rel(rel):
        wait("pe", rel[0], rel[1])

    E_x2 = {}
    war = {"tta": 0, "p": 0}

    E_xn1_d = {}
    E_xn1t_fill_prev = [0]

    def ln1_part1(p, jj):
        xtj = x_sb[:, (4 * p + jj) * 1024:(4 * p + jj + 1) * 1024]
        xoj = xn1_bf[:, jj * 1024:(jj + 1) * 1024]
        if jj == 0:
            wait("se", *E_xp[p])
            wait("se", "pe", E_xn1t_fill_prev[0])
        op("se", lambda E, xtj=xtj, xoj=xoj: E.activation(
            xoj, xtj, ACT.Identity, accum_out=sums[:, jj:jj + 1]),
            inc="se")
        wait("se", "se", n["se"])
        op("se", lambda E, xtj=xtj, xoj=xoj: E.activation(
            xoj, xtj, ACT.Square, accum_out=sqsums[:, jj:jj + 1]),
            inc="se")

    def ln1_part2(p):
        wait("ve", "se", n["se"])
        for b_ in [
            lambda E: E.tensor_single_scalar(
                nmean[:, :4], sums[:, :4], -1.0 / D, ALU.mult),
            lambda E: E.tensor_single_scalar(
                mmean[:, :4], sums[:, :4], 1.0 / D, ALU.mult),
            lambda E: E.tensor_mul(mmean[:, :4], mmean[:, :4], mmean[:, :4]),
            lambda E: E.scalar_tensor_tensor(
                veps[:, :4], sqsums[:, :4], 1.0 / D, mmean[:, :4],
                ALU.mult, ALU.subtract),
            lambda E: E.tensor_single_scalar(
                veps[:, :4], veps[:, :4], EPS, ALU.add),
        ]:
            op("ve", b_, inc="ve")
            wait("ve", "ve", n["ve"])
        wait("se", "ve", n["ve"])
        op("se", lambda E: E.activation(vtmp[:, :4], veps[:, :4], ACT.Ln),
           inc="se")
        wait("se", "se", n["se"])
        op("se", lambda E: E.activation(rstd[:, :4], vtmp[:, :4], ACT.Exp,
                                        scale=-0.5), inc="se")

    def ln1_part3(p):
        wait("ve", "se", n["se"])
        wait("ve", "pe", E_xn1t_fill_prev[0])
        for jj in range(4):
            xtj = x_sb[:, (4 * p + jj) * 1024:(4 * p + jj + 1) * 1024]
            xoj = xn1_bf[:, jj * 1024:(jj + 1) * 1024]
            op("ve", lambda E, jj=jj, xtj=xtj, xoj=xoj: E.tensor_scalar(
                xoj, xtj, nmean[:, jj:jj + 1], rstd[:, jj:jj + 1],
                ALU.add, ALU.mult), inc="ve" if jj == 3 else None)
        E_xn1_d[p] = n["ve"]

    for jj in range(4):
        ln1_part1(0, jj)
    ln1_part2(0)
    ln1_part3(0)

    E_xn1t_d = {}

    def xn1t_group(p, jj):
        if jj == 0:
            wait("pe", "ve", E_xn1_d[p])
            if p == 0:
                wait("pe", "gp", E_ident)
        pe_wait_rel(relD[jj % 2])
        for c in range(8):
            op("pe", lambda E, jj=jj, c=c: E.transpose(
                psD[:, (jj % 2) * 1024 + c * 128:
                    (jj % 2) * 1024 + (c + 1) * 128],
                xn1_bf[:, jj * 1024 + c * 128:jj * 1024 + (c + 1) * 128],
                ident[:]), inc="pe" if c == 7 else None)
        fill = n["pe"]
        wait("se", "pe", fill)
        dst = bass.AP(xn1t_sb[:].tensor, jj * 128,
                      [[4096, 128], [512, 8], [1, 128]])
        src = psD[:, (jj % 2) * 1024:(jj % 2) * 1024 + 1024].rearrange(
            "p (c t) -> p c t", c=8)
        op("se", lambda E, dst=dst, src=src: E.copy(dst, src),
           inc="se")
        relD[jj % 2] = ("se", n["se"])
        E_xn1t_d[(p, jj)] = n["se"]
        if jj == 3:
            E_xn1t_fill_prev[0] = fill

    for p in range(PA):
        if p == 0:
            for jj in range(4):
                xn1t_group(0, jj)

        E_qe, E_ke = [0] * 8, [0] * 8
        wait("pe", "se", E_xn1t_d[(p, 3)])
        if p == 0:
            wait("pe", *E_wq_half)
        g = 0
        for oc in range(8):
            if p == 0 and oc == 4:
                wait("pe", *E_w["q"])
            qb = qkv_quarters[g % 4]
            pe_wait_rel(relQ[g % 4])
            for c in range(8):
                op("pe", lambda E, oc=oc, c=c, qb=qb: E.matmul(
                    qb, wq_sb[:, c * 1024 + oc * 128:c * 1024 + (oc + 1) * 128],
                    xn1t_sb[:, c * 512:(c + 1) * 512],
                    start=(c == 0), stop=(c == 7)),
                    inc="pe" if c == 7 else None)
            wait("se", "pe", n["pe"])
            if p == 0 and oc == 0:
                wait("se", "ve", E_bqs)
            op("se", lambda E, oc=oc, qb=qb: E.activation(
                q_bf[:, oc * 512:(oc + 1) * 512], qb, ACT.Identity,
                bias=bqs_sb[:, oc:oc + 1], scale=DKS), inc="se")
            relQ[g % 4] = ("se", n["se"])
            E_qe[oc] = n["se"]
            g += 1
        if p == 0:
            wait("pe", *E_w["k"])
        for oc in range(8):
            qb = qkv_quarters[g % 4]
            pe_wait_rel(relQ[g % 4])
            for c in range(8):
                op("pe", lambda E, oc=oc, c=c, qb=qb: E.matmul(
                    qb, wk_sb[:, c * 1024 + oc * 128:c * 1024 + (oc + 1) * 128],
                    xn1t_sb[:, c * 512:(c + 1) * 512],
                    start=(c == 0), stop=(c == 7)),
                    inc="pe" if c == 7 else None)
            wait("se", "pe", n["pe"])
            op("se", lambda E, oc=oc, qb=qb: E.activation(
                k_bf[:, oc * 512:(oc + 1) * 512], qb, ACT.Identity,
                bias=bk_sb[:, oc:oc + 1]), inc="se")
            relQ[g % 4] = ("se", n["se"])
            E_ke[oc] = n["se"]
            g += 1
        if p == 0:
            wait("pe", *E_w["v"])
        for jj in range(4):
            for half in range(2):
                qb = qkv_quarters[g % 4]
                pe_wait_rel(relQ[g % 4])
                for c in range(8):
                    op("pe", lambda E, jj=jj, half=half, c=c, qb=qb: E.matmul(
                        qb,
                        xn1t_sb[:, c * 512 + jj * 128:c * 512 + (jj + 1) * 128],
                        wv_sb[:, c * 1024 + half * 512:
                              c * 1024 + (half + 1) * 512],
                        start=(c == 0), stop=(c == 7)),
                        inc="pe" if c == 7 else None)
                wait("ve", "pe", n["pe"])
                op("ve", lambda E, jj=jj, half=half, qb=qb: E.tensor_copy(
                    v_bf[:, jj * 1024 + half * 512:
                         jj * 1024 + (half + 1) * 512], qb), inc="ve")
                relQ[g % 4] = ("ve", n["ve"])
                g += 1
        E_v_all = n["ve"]

        E_exp4 = [0] * 8
        E_p = [0] * 8
        E_ptev = [0] * 8
        E_pt = [0] * 8
        E_pv = [0] * 8
        E_attnev = [0] * 8

        def scores(h):
            wait("pe", "se", max(E_qe[h], E_ke[h]))
            if h >= 2:
                wait("pe", "se", E_exp4[h - 2])
            else:
                pe_wait_rel(relC2[h % 2])
            for bl in range(8):
                op("pe", lambda E, h=h, bl=bl: E.matmul(
                    psC[(bl // 4) * 64:(bl // 4) * 64 + 64,
                        (h % 2) * 512 + (bl % 4) * 64:
                        (h % 2) * 512 + (bl % 4) * 64 + 64],
                    q_bf[:, h * 512 + bl * 64:h * 512 + (bl + 1) * 64],
                    k_bf[:, h * 512 + bl * 64:h * 512 + (bl + 1) * 64],
                    start=True, stop=True), inc="pe" if bl == 7 else None)
            return n["pe"]

        def exp4_h(h, E_s):
            wait("se", "pe", E_s)
            c0 = (h % 3) * 4
            for g in range(4):
                op("se", lambda E, h=h, g=g, c0=c0: E.activation(
                    s_f32[h % 3][:, g * 64:(g + 1) * 64],
                    psC[:, (h % 2) * 512 + g * 64:
                        (h % 2) * 512 + (g + 1) * 64],
                    ACT.Exp, accum_out=ssum[:, c0 + g:c0 + g + 1]),
                    inc="se" if g == 3 else None)
            E_exp4[h] = n["se"]

        def softmax_post(h):
            c0 = (h % 3) * 4
            wait("ve", "se", E_exp4[h])
            if h >= 3:
                wait("ve", "pe", E_pt[h - 3])
            op("ve", lambda E, c0=c0: E.reciprocal(
                rsum[:, c0:c0 + 4], ssum[:, c0:c0 + 4]), inc="ve")
            wait("ve", "ve", n["ve"])
            g3 = s_f32[h % 3][:].rearrange("p (g k) -> p g k", g=4)
            op("ve", lambda E, h=h, g3=g3, c0=c0: E.tensor_tensor(
                p_bf[h % 3][:].rearrange("p (g k) -> p g k", g=4), g3,
                rsum[:, c0:c0 + 4].unsqueeze(2).broadcast_to([128, 4, 64]),
                ALU.mult), inc="ve")
            E_p[h] = n["ve"]

        def pt_h(h):
            wait("pe", "ve", E_p[h])
            pe_wait_rel(relD[h % 2])
            base = (h % 2) * 1024
            for t_ in range(2):
                op("pe", lambda E, h=h, t_=t_, base=base: E.transpose(
                    psD[:, base + t_ * 128:base + (t_ + 1) * 128],
                    p_bf[h % 3][:, t_ * 128:(t_ + 1) * 128], ident[:]),
                    inc="pe" if t_ == 1 else None)
            E_pt[h] = n["pe"]

        def ptev_h(h):
            wait("se", "pe", E_pt[h])
            base = (h % 2) * 1024
            op("se", lambda E, h=h, base=base: E.copy(
                pt_bf[h % 3][:], psD[:, base:base + 256]), inc="se")
            E_ptev[h] = n["se"]
            relD[h % 2] = ("se", n["se"])

        def pv_h(h):
            wait("pe", "se", E_ptev[h])
            if h < 2:
                wait("pe", "ve", E_v_all)
                wait("pe", "se", max(E_qe[7], E_ke[7]))
            else:
                wait("pe", "ve", E_attnev[h - 2])
            pair = psA if h % 2 == 0 else psB
            for bl in range(8):
                panel = bl % 4
                tt = panel // 2
                kb = (panel % 2) * 64
                qc = (bl // 4) * 64
                col = (bl % 2) * 512 + (bl // 2) * 64
                op("pe", lambda E, h=h, bl=bl, tt=tt, kb=kb, qc=qc, col=col,
                   pair=pair: E.matmul(
                       pair[:, col:col + 64],
                       v_bf[(bl % 2) * 64:(bl % 2) * 64 + 64,
                            (bl // 2) * 1024 + h * 128:
                            (bl // 2) * 1024 + (h + 1) * 128],
                       pt_bf[h % 3][kb:kb + 64,
                                    tt * 128 + qc:tt * 128 + qc + 64],
                       start=True, stop=True),
                    inc="pe" if bl == 7 else None)
            E_pv[h] = n["pe"]

        def attnev_h(h):
            wait("ve", "pe", E_pv[h])
            pair = psA if h % 2 == 0 else psB
            src = bass.AP(pair[:].tensor, 0,
                          [[1024, 128], [512, 2], [64, 4], [1, 64]])
            dst = bass.AP(attn_sb[:].tensor, h * 512,
                          [[4096, 128], [64, 2], [128, 4], [1, 64]])
            op("ve", lambda E, h=h, src=src, dst=dst: E.tensor_scalar(
                dst, src, bv_sb[:, h:h + 1], None, ALU.add), inc="ve")
            E_attnev[h] = n["ve"]

        def w1_group(fc):
            half = w2g["gidx"] % 2
            pe_wait_rel(relD[half])
            for c in range(8):
                op("pe", lambda E, fc=fc, c=c, half=half: E.transpose(
                    psD[:, half * 1024 + c * 128:half * 1024 + (c + 1) * 128],
                    wtmp1[fc % 4][:, c * 128:(c + 1) * 128], ident[:]),
                    inc="pe" if c == 7 else None)
            fill = n["pe"]
            E_w1tr[fc] = n["pe"]
            if fc + 4 < 32:
                w1_dma(fc + 4)
            wait("se", "pe", fill)
            if w2g["gidx"] >= 2:
                wait("se", *E_st2[w2g["gidx"] - 2])
            op("se", lambda E, half=half, s=w2g["gidx"] % 2: E.copy(
                w2stage[s][:], psD[:, half * 1024:half * 1024 + 1024]),
                inc="se")
            relD[half] = ("se", n["se"])
            wait("se", "se", n["se"])
            dst = bass.AP(w1d_h.tensor, fc * 131072, [[1024, 128], [1, 1024]])
            E_st2[w2g["gidx"]] = dma("se", lambda E, dst=dst,
                                     s=w2g["gidx"] % 2:
                                     E.dma_start(out=dst, in_=w2stage[s][:]))
            w2g["gidx"] += 1

        def w2_group(ci, grp):
            half = w2g["gidx"] % 2
            pe_wait_rel(relD[half])
            for t_ in range(8):
                tl = grp * 8 + t_
                op("pe", lambda E, ci=ci, tl=tl, half=half: E.transpose(
                    psD[:, half * 1024 + (tl % 8) * 128:
                        half * 1024 + (tl % 8 + 1) * 128],
                    wtmp2[ci % 2][:, tl * 128:(tl + 1) * 128], ident[:]),
                    inc="pe" if t_ == 7 else None)
            fill = n["pe"]
            m, fh = chunks2[ci]
            wait("ve", "pe", fill)
            if w2g["gidx"] >= 2:
                wait("ve", *E_st2[w2g["gidx"] - 2])
            op("ve", lambda E, half=half, s=w2g["gidx"] % 2: E.tensor_copy(
                w2stage[s][:], psD[:, half * 1024:half * 1024 + 1024]),
                inc="ve")
            relD[half] = ("ve", n["ve"])
            base = (fh * 16 + grp * 8) * 131072 + m * 128
            dst = bass.AP(w2d_h.tensor, base,
                          [[1024, 128], [131072, 8], [1, 128]])
            wait("sy", "ve", n["ve"])
            E_st2[w2g["gidx"]] = dma("sy", lambda E, dst=dst,
                                     s=w2g["gidx"] % 2:
                                     E.dma_start(out=dst, in_=w2stage[s][:]))
            w2g["gidx"] += 1
            if grp == 1:
                E_tr2[ci] = n["pe"]
                if ci + 2 < len(chunks2):
                    w2_dma(ci + 2)

        E_s = scores(0)
        exp4_h(0, E_s)
        E_s = scores(1)
        exp4_h(1, E_s)
        softmax_post(0)
        pt_h(0)
        ptev_h(0)
        E_s = scores(2)
        exp4_h(2, E_s)
        softmax_post(1)
        pt_h(1)
        ptev_h(1)
        for h in range(8):
            if h + 2 < 8:
                softmax_post(h + 2)
                pt_h(h + 2)
                ptev_h(h + 2)
            if h + 3 < 8:
                E_s = scores(h + 3)
                exp4_h(h + 3, E_s)
            pv_h(h)
            attnev_h(h)
            ci = 4 * p + h // 2
            if h % 2 == 0:
                wait("pe", *E_ld2[ci])
            w2_group(ci, h % 2)
            fcw = 8 * p + h
            wait("pe", *E_w1ld[fcw])
            w1_group(fcw)
            if p + 1 < PA:
                if 1 <= h <= 4:
                    ln1_part1(p + 1, h - 1)
                elif h == 5:
                    ln1_part2(p + 1)
                elif h == 6:
                    ln1_part3(p + 1)
        relC2[0] = ("se", E_exp4[6])
        relC2[1] = ("se", E_exp4[7])

        if p == 0:
            wait("pe", *E_bo)
            wait("pe", *E_w["o"])
            wait("pe", "gp", E_ones)
        if p + 1 < PA:
            xn1t_group(p + 1, 0)
            xn1t_group(p + 1, 1)
        wait("pe", "ve", E_attnev[7])
        for jj in range(4):
            pair = psA if jj % 2 == 0 else psB
            if jj >= 2:
                wait("pe", "ve", E_x2[4 * p + jj - 2])
            for half in range(2):
                for c in range(8):
                    op("pe", lambda E, jj=jj, half=half, c=c, pair=pair:
                       E.matmul(
                           pair[:, half * 512:(half + 1) * 512],
                           attn_sb[:, c * 512 + jj * 128:
                                   c * 512 + (jj + 1) * 128],
                           wo_sb[:, c * 1024 + half * 512:
                                 c * 1024 + (half + 1) * 512],
                           start=(c == 0), stop=False))
                op("pe", lambda E, half=half, pair=pair: E.matmul(
                    pair[:, half * 512:(half + 1) * 512], ones_bf[:],
                    bo_bf[0:1, half * 512:(half + 1) * 512],
                    start=False, stop=True), inc="pe" if half == 1 else None)
            E_opj = n["pe"]
            if p + 1 < PA and jj < 2:
                xn1t_group(p + 1, jj + 2)
            wait("ve", "pe", E_opj)
            jg = 4 * p + jj
            for half in range(2):
                xs = x_sb[:, jg * 1024 + half * 512:jg * 1024 + (half + 1) * 512]
                op("ve", lambda E, xs=xs, pair=pair, half=half:
                   E.scalar_tensor_tensor(
                       xs, pair[:, half * 512:(half + 1) * 512], 1.0, xs,
                       ALU.mult, ALU.add), inc="ve" if half == 1 else None)
            E_x2[jg] = n["ve"]
        relQ[0] = relQ[1] = ("ve", E_x2[4 * p + 2])
        relQ[2] = relQ[3] = ("ve", E_x2[4 * p + 3])

    E_x2_all = n["ve"]
    E_attn_end_pe = n["pe"]

    regC.close()
    regB.close()
    E_w2_all = [(f"qe{i}", n[f"qe{i}"]) for i in range(NSE)]

    xn2_bf = [e(nc.sbuf_tensor(f"xn2_bf_{i}", [128, 1024], BF16))
              for i in range(2)]
    xn2t_sb = e(nc.sbuf_tensor("xn2t_sb", [128, 2048], BF16))
    h_bf = [e(nc.sbuf_tensor(f"h_bf_{i}", [128, 256], BF16)) for i in range(2)]
    w2slab = [e(nc.sbuf_tensor(f"w2slab_{i}", [128, 2048], BF16))
              for i in range(4)]
    w1slab = [e(nc.sbuf_tensor(f"w1slab_{i}", [128, 2048], BF16))
              for i in range(4)]
    y_sb = e(nc.sbuf_tensor("y_sb", [128, 2048], F32))
    mem_q = e(nc.sbuf_tensor("mem_q", [128, 2048], F32))
    spk_q = e(nc.sbuf_tensor("spk_q", [128, 2048], BF16))
    acc_q = e(nc.sbuf_tensor("acc_q", [128, 2048], BF16))

    relH = relC2
    relD6 = [relD[0]]
    relD7 = [relD[1]]
    E_slab1 = {}

    def slab1_load(p, pr, prev_fc):
        if p == 0 and pr == 0:
            for sm, vv in E_w2_all:
                wait("sy", sm, vv)
        if prev_fc:
            wait("sy", "pe", prev_fc)
        ev = dma("sy", lambda E, pr=pr: E.dma_start(
            out=w1slab[pr % 4][:].rearrange("p (f c) -> p f c", f=2),
            in_=bass.AP(w1d_h.tensor, pr * 262144,
                        [[1024, 128], [131072, 2], [1, 1024]])))
        E_slab1[(p, 2 * pr)] = ev
        E_slab1[(p, 2 * pr + 1)] = ev

    E_yev = {}
    E_spk_yfree = {}
    E_spk_vdone = {}
    E_spk_gdone = {}
    E_final = {}
    E_accT = {}
    E_w1ld = {}
    E_w1ev = {}
    E_xn2t = {}
    E_norm2_d = {}
    E_proj_pe = {}
    E_slab = {}
    E_xn2t_fill = [0, 0]

    def ln2_tile(p, jj):
        jg = 2 * p + jj
        xs = x_sb[:, jg * 1024:(jg + 1) * 1024]
        if p == 0 and jj == 0:
            wait("ve", "ve", E_x2_all)
            wait("se", "ve", E_x2_all)
        op("ve", lambda E, xs=xs: E.reduce_sum(sums[:, 0:1], xs, AX.X),
           inc="ve")
        E_red = n["ve"]
        wait("se", "pe", E_xn2t_fill[jj])
        op("se", lambda E, xs=xs, jj=jj: E.activation(
            xn2_bf[jj][:], xs, ACT.Square, accum_out=sqsums[:, 0:1]),
            inc="se")
        E_sq2 = n["se"]
        wait("ve", "ve", E_red)
        wait("ve", "se", E_sq2)
        for b_ in [
            lambda E: E.tensor_single_scalar(
                nmean[:, 0:1], sums[:, 0:1], -1.0 / D, ALU.mult),
            lambda E: E.tensor_single_scalar(
                mmean[:, 0:1], sums[:, 0:1], 1.0 / D, ALU.mult),
            lambda E: E.tensor_mul(mmean[:, 0:1], mmean[:, 0:1],
                                   mmean[:, 0:1]),
            lambda E: E.scalar_tensor_tensor(
                veps[:, 0:1], sqsums[:, 0:1], 1.0 / D, mmean[:, 0:1],
                ALU.mult, ALU.subtract),
            lambda E: E.tensor_single_scalar(
                veps[:, 0:1], veps[:, 0:1], EPS, ALU.add),
        ]:
            op("ve", b_, inc="ve")
            wait("ve", "ve", n["ve"])
        E_veps2 = n["ve"]
        wait("se", "ve", E_veps2)
        op("se", lambda E: E.activation(vtmp[:, 0:1], veps[:, 0:1],
                                        ACT.Ln), inc="se")
        wait("se", "se", n["se"])
        op("se", lambda E: E.activation(rstd[:, 0:1], vtmp[:, 0:1],
                                        ACT.Exp, scale=-0.5), inc="se")
        E_rstd2 = n["se"]
        wait("ve", "se", E_rstd2)
        op("ve", lambda E, xs=xs, jj=jj: E.tensor_scalar(
            xn2_bf[jj][:], xs, nmean[:, 0:1], rstd[:, 0:1],
            ALU.add, ALU.mult), inc="ve")
        E_norm2_d[(p, jj)] = n["ve"]

    def ln2_transposes(p, jj):
        wait("pe", "ve", E_norm2_d[(p, jj)])
        pe_wait_rel(relD6[0])
        for c in range(8):
            op("pe", lambda E, jj=jj, c=c: E.transpose(
                psD[:, c * 128:(c + 1) * 128],
                xn2_bf[jj][:, c * 128:(c + 1) * 128], ident[:]),
                inc="pe" if c == 7 else None)
        E_xn2t_fill[jj] = n["pe"]
        wait("se", "pe", E_xn2t_fill[jj])
        dst = bass.AP(xn2t_sb[:].tensor, jj * 128,
                      [[2048, 128], [256, 8], [1, 128]])
        src2 = psD[:, 0:1024].rearrange("p (c t) -> p c t", c=8)
        op("se", lambda E, dst=dst, src2=src2: E.copy(dst, src2),
           inc="se")
        relD6[0] = ("se", n["se"])
        E_xn2t[p] = n["se"]

    def slab_load(p, pr, prev_proj):
        if p == 0 and pr == 0:
            for sm, vv in E_w2_all:
                wait("sy", sm, vv)
        if prev_proj:
            wait("sy", "pe", prev_proj)
        ev = dma("sy", lambda E, pr=pr: E.dma_start(
            out=w2slab[pr % 4][:].rearrange("p (f c) -> p f c", f=2),
            in_=bass.AP(w2d_h.tensor, pr * 262144,
                        [[1024, 128], [131072, 2], [1, 1024]])))
        E_slab[(p, 2 * pr)] = ev
        E_slab[(p, 2 * pr + 1)] = ev

    ln2_tile(0, 0)
    ln2_tile(0, 1)
    for pr in range(4):
        slab_load(0, pr, 0)
        slab1_load(0, pr, 0)

    def emit_accT(pp, gd=None):
        for jj in range(2):
            wait("pe", "gp", E_spk_gdone[pp] if gd is None else gd[jj])
            prev = E_final.get((pp, jj - 1), E_final.get((pp - 1, 1), 0))
            wait("pe", "ve", prev)
            for c in range(8):
                op("pe", lambda E, jj=jj, c=c: E.transpose(
                    psD[:, 1024 + c * 128:1024 + (c + 1) * 128],
                    acc_q[:, c * 256 + jj * 128:c * 256 + (jj + 1) * 128],
                    ident[:]), inc="pe" if c == 7 else None)
            fill = n["pe"]
            jg = 2 * pp + jj
            xs = x_sb[:, jg * 1024:(jg + 1) * 1024]
            wait("ve", "pe", fill)
            op("ve", lambda E, xs=xs: E.scalar_tensor_tensor(
                xs, psD[:, 1024:2048], 0.2, xs, ALU.mult, ALU.add), inc="ve")
            E_final[(pp, jj)] = n["ve"]
            wait("gp", "ve", n["ve"])
            dma("gp", lambda E, xs=xs, jg=jg: E.dma_start(
                out=out_hbm[jg * 128:(jg + 1) * 128, :], in_=xs))
        E_accT[pp] = n["pe"]

    for p in range(PF):
        E_sqf = {}
        ln2_transposes(p, 0)
        ln2_transposes(p, 1)
        wait("pe", "se", E_xn2t[p])
        for step in range(33):
            fc = step
            if fc < 32:
                eng, cnt = relH[fc % 2]
                if cnt > 0:
                    wait("pe", eng, cnt)
                wait("pe", *E_slab1[(p, fc)])
                hp = psC[:, (fc % 2) * 512:(fc % 2) * 512 + 256]
                s1b = w1slab[(fc // 2) % 4][:, (fc % 2) * 1024:
                                            (fc % 2) * 1024 + 1024]
                for c in range(8):
                    op("pe", lambda E, fc=fc, c=c, hp=hp, s1b=s1b: E.matmul(
                        hp, s1b[:, c * 128:(c + 1) * 128],
                        xn2t_sb[:, c * 256:(c + 1) * 256],
                        start=(c == 0), stop=(c == 7)),
                        inc="pe" if c == 7 else None)
                E_fcf = n["pe"]
                if fc % 2 == 1 and fc // 2 + 4 < 16:
                    slab1_load(p, fc // 2 + 4, E_fcf)
                if p + 1 < PF and fc in (24, 28):
                    ln2_tile(p + 1, (fc - 24) // 4)
                wait("se", "pe", E_fcf)
                op("se", lambda E, fc=fc, hp=hp: E.activation(
                    h_bf[fc % 2][:], hp, ACT.Relu), inc="se")
                wait("se", "se", n["se"])
                op("se", lambda E, fc=fc: E.activation(
                    h_bf[fc % 2][:], h_bf[fc % 2][:], ACT.Square), inc="se")
                E_sqf[fc] = n["se"]
                relH[fc % 2] = ("se", E_sqf[fc])
            if step >= 1:
                fp = step - 1
                wait("pe", "se", E_sqf[fp])
                wait("pe", *E_slab[(p, fp)])
                if fp == 0:
                    if p > 0:
                        for q in range(4):
                            wait("pe", "se", E_yev[(p - 1, q)])
                    else:
                        wait("pe", "ve", E_x2_all)
                slb = w2slab[(fp // 2) % 4][:, (fp % 2) * 1024:
                                            (fp % 2) * 1024 + 1024]
                for m in range(8):
                    tgt = psA if m < 4 else psB
                    mc = m % 4
                    op("pe", lambda E, fp=fp, m=m, tgt=tgt, mc=mc, slb=slb:
                       E.matmul(
                           tgt[:, mc * 256:(mc + 1) * 256],
                           slb[:, m * 128:(m + 1) * 128],
                           h_bf[fp % 2][:],
                           start=(fp == 0 and m % 2 == 0),
                           stop=(fp == 31 and m % 2 == 1)),
                       inc="pe" if m == 7 else None)
                E_proj_pe[(p, fp)] = n["pe"]
                if fp % 2 == 1 and fp // 2 + 4 < 16:
                    slab_load(p, fp // 2 + 4, E_proj_pe[(p, fp)])
        E_y_fill = n["pe"]
        if p + 1 < PF:
            for j in range(4):
                slab_load(p + 1, j, E_proj_pe[(p, min(25 + 2 * j, 31))])
                slab1_load(p + 1, j, E_y_fill)
        if p > 0:
            emit_accT(p - 1)

        for q in range(4):
            ps = psA if q < 2 else psB
            half = q % 2
            wait("se", "pe", E_y_fill)
            if (p - 1) in E_spk_yfree:
                wait("se", "ve", E_spk_yfree[p - 1])
            op("se", lambda E, ps=ps, half=half, q=q: E.copy(
                y_sb[:, q * 512:(q + 1) * 512],
                ps[:, half * 512:(half + 1) * 512]), inc="se")
            E_yev[(p, q)] = n["se"]

        wait("ve", "se", E_yev[(p, 3)])
        if (p - 1) in E_spk_vdone:
            wait("ve", "ve", E_spk_vdone[p - 1])
        if (p - 1) in E_spk_gdone:
            wait("ve", "gp", E_spk_gdone[p - 1])

        def vop(build):
            op("ve", build, inc="ve")
            wait("ve", "ve", n["ve"])

        nhalf = 2 if p == PF - 1 else 1
        E_spk_gd_half = []

        def ap3(t, hf):
            return bass.AP(t[:].tensor, hf * 128,
                           [[2048, 128], [256, 8], [1, 128]])

        for hf in range(nhalf):
            if nhalf == 1:
                yv, sv, mv, av = y_sb[:], spk_q[:], mem_q[:], acc_q[:]
            else:
                yv, sv, mv, av = (ap3(y_sb, hf), ap3(spk_q, hf),
                                  ap3(mem_q, hf), ap3(acc_q, hf))
            vop(lambda E, yv=yv, sv=sv: E.tensor_single_scalar(
                sv, yv, 1.0, ALU.is_gt))
            if hf == 0 and (p - 1) in E_accT:
                wait("gp", "pe", E_accT[p - 1])
            wait("gp", "gp", n["gp"])
            wait("gp", "ve", n["ve"])
            op("gp", lambda E, sv=sv, av=av: E.tensor_copy(av, sv), inc="gp")
            vop(lambda E, yv=yv, sv=sv, mv=mv: E.scalar_tensor_tensor(
                mv, yv, 1.9, sv, ALU.mult, ALU.subtract))
            for st3 in range(3):
                wait("ve", "gp", n["gp"])
                vop(lambda E, sv=sv, mv=mv: E.tensor_single_scalar(
                    sv, mv, 1.0, ALU.is_gt))
                wait("gp", "gp", n["gp"])
                wait("gp", "ve", n["ve"])
                op("gp", lambda E, sv=sv, av=av: E.tensor_add(av, av, sv),
                   inc="gp")
                vop(lambda E, sv=sv, mv=mv: E.scalar_tensor_tensor(
                    mv, mv, 0.9, sv, ALU.mult, ALU.subtract))
                vop(lambda E, yv=yv, mv=mv: E.scalar_tensor_tensor(
                    mv, mv, 1.0, yv, ALU.mult, ALU.add))
                if st3 == 2 and hf == nhalf - 1:
                    E_spk_yfree[p] = n["ve"]
            wait("ve", "gp", n["gp"])
            vop(lambda E, sv=sv, mv=mv: E.tensor_single_scalar(
                sv, mv, 1.0, ALU.is_gt))
            E_spk_vdone[p] = n["ve"]
            wait("gp", "gp", n["gp"])
            wait("gp", "ve", n["ve"])
            op("gp", lambda E, sv=sv, av=av: E.tensor_add(av, av, sv),
               inc="gp")
            E_spk_gd_half.append(n["gp"])
        E_spk_gdone[p] = n["gp"]

        if p == PF - 1:
            emit_accT(p, E_spk_gd_half)

    for i in range(NSY):
        wait("sy", f"qs{i}", n[f"qs{i}"])
    for i in range(NGP):
        wait("sy", f"qg{i}", n[f"qg{i}"])
    for i in range(NSE):
        wait("sy", f"qe{i}", n[f"qe{i}"])

    with nc.Block() as block:
        @block.tensor
        def _(E):
            for fn in plan["pe"]:
                fn(E)

        @block.vector
        def _(E):
            for fn in plan["ve"]:
                fn(E)

        @block.scalar
        def _(E):
            for fn in plan["se"]:
                fn(E)

        @block.gpsimd
        def _(E):
            for fn in plan["gp"]:
                fn(E)

        @block.sync
        def _(E):
            for fn in plan["sy"]:
                fn(E)

    ctx.close()
    return nc


_NC_CACHE = None


def get_nc():
    global _NC_CACHE
    if _NC_CACHE is None:
        _NC_CACHE = build_nc()
    return _NC_CACHE


def kernel(L0_attn_src, L0_attn_Wq, L0_attn_bq, L0_attn_Wk, L0_attn_bk,
           L0_attn_Wv, L0_attn_bv, L0_attn_Wo, L0_attn_bo,
           L0_ffn_W_fc, L0_ffn_W_proj):
    nc = get_nc()
    x = np.ascontiguousarray(np.asarray(L0_attn_src, dtype=np.float32))
    common = {
        "Wq": np.ascontiguousarray(np.asarray(L0_attn_Wq, np.float32)),
        "Wk": np.ascontiguousarray(np.asarray(L0_attn_Wk, np.float32)),
        "Wv": np.ascontiguousarray(np.asarray(L0_attn_Wv, np.float32)),
        "Wo": np.ascontiguousarray(np.asarray(L0_attn_Wo, np.float32)),
        "bq": np.ascontiguousarray(np.asarray(L0_attn_bq, np.float32)),
        "bk": np.ascontiguousarray(np.asarray(L0_attn_bk, np.float32)),
        "bv": np.ascontiguousarray(np.asarray(L0_attn_bv, np.float32)),
        "bo": np.ascontiguousarray(np.asarray(L0_attn_bo, np.float32)),
        "Wfc": np.ascontiguousarray(np.asarray(L0_ffn_W_fc, np.float32)),
        "Wpr": np.ascontiguousarray(np.asarray(L0_ffn_W_proj, np.float32)),
    }
    in_maps = [dict(common, x=np.ascontiguousarray(x[c])) for c in range(8)]
    res = run_bass_kernel_spmd(nc, in_maps, core_ids=list(range(8)))
    return np.stack([res.results[c]["out"] for c in range(8)], axis=0)
